# version 24
# baseline (speedup 1.0000x reference)
import numpy as np
import ml_dtypes
from contextlib import ExitStack

import concourse.bass as bass
import concourse.mybir as mybir
from concourse.bass_utils import run_bass_kernel_spmd

F32 = mybir.dt.float32
BF16 = mybir.dt.bfloat16
FP16 = mybir.dt.float16
FP8 = mybir.dt.float8e4
AF = mybir.ActivationFunctionType
ALU = mybir.AluOpType
AX = mybir.AxisListType

NCORES = 8
B, S, D = 16, 2048, 1024
NH, HD = 16, 64
HFF = 4096
WINDOW = 256
SEQ_PER_CORE = B // NCORES
NT = S // 128
NC = D // 128
TH = 1024

ENGS = ("pe", "act", "dve", "sync", "gdma", "pool")
PHYS = {"pe": "tensor", "act": "scalar", "dve": "vector",
        "sync": "sync", "gdma": "gpsimd", "pool": "gpsimd"}
DMAQ = ("sync", "gdma")

DEBUG_TAPS = False
DRAIN_GUARD = False


class Bld:

    def __init__(self, nc, sems):
        self.nc = nc
        self.recs = {e: [] for e in ENGS}
        self.order = []
        self.cnt = {e: 0 for e in ENGS}
        self.wr = {}
        self.rd = {}
        self.sems = sems
        self.final_deps = []
        self.dummy = {}
        self.ndummy = {e: 0 for e in ENGS}
        self.chans = {}

    def op(self, eng, emit_fn, reads=(), writes=(), chan=None):
        assert (chan is not None) == (eng in DMAQ), (eng, chan)
        if chan is not None:
            chan = (eng,) + chan
        deps = {}
        def add(p, c, raw):
            e = deps.setdefault(p, [-1, -1])
            k = 0 if raw else 1
            e[k] = max(e[k], c)
        for r in reads:
            w = self.wr.get(r)
            if w is not None:
                add(w[0], w[1], True)
        for w_ in writes:
            w = self.wr.get(w_)
            if w is not None:
                add(w[0], w[1], True)
            for re_, c in self.rd.get(w_, {}).items():
                add(re_, c, False)
        for p, (craw, cwar) in deps.items():
            if (DRAIN_GUARD and p != eng and p not in DMAQ and craw >= cwar
                    and craw == self.cnt[p] - 1 and p in self.dummy):
                self.ndummy[p] += 1
                dfn, drd = self.dummy[p]
                self.op(p, dfn, reads=drd, writes=[("FNC", p)])
        myc = self.cnt[eng]
        dl = []
        for p, (craw, cwar) in deps.items():
            c = max(craw, cwar)
            if p == eng:
                if eng != "pe":
                    dl.append((p, c))
            else:
                if DRAIN_GUARD and p not in DMAQ and craw >= cwar:
                    dl.append((p, craw, True))
                else:
                    dl.append((p, c))
        self.recs[eng].append((emit_fn, dl, chan))
        self.order.append((eng, myc))
        self.cnt[eng] = myc + 1
        if chan is not None:
            self.chans[chan] = self.chans.get(chan, 0) + 1
        for r in reads:
            self.rd.setdefault(r, {})[eng] = myc
        for w_ in writes:
            self.wr[w_] = (eng, myc)
            self.rd[w_] = {}

    def raw_wait(self, eng, prod, idx=None):
        if idx is None:
            idx = self.cnt[prod] - 1
        self.final_deps.append((eng, prod, idx))

    def finalize(self, chan_sems):
        ncnt = {e: len(self.recs[e]) for e in ENGS}
        chpos = {}
        chtot = {c: 0 for c in self.chans}
        for e, i in self.order:
            ch = self.recs[e][i][2]
            if ch is not None:
                chtot[ch] += 1
                chpos[(e, i)] = (ch, chtot[ch])
        noshift = set()
        emipos = {}
        for n, (e, i) in enumerate(self.order):
            emipos[(e, i)] = n

        def eff(eng, idx, dep):
            p = dep[0]
            if (len(dep) == 3 and dep[2] and dep[1] + 1 < ncnt[p]
                    and emipos[(p, dep[1] + 1)] < emipos[(eng, idx)]
                    and (eng, idx, dep[1]) not in noshift):
                return (p, dep[1] + 1)
            return (p, dep[1])

        def mk_schedule():
            targets = {e: set() for e in ENGS}
            for e in ENGS:
                for i, (_, dl, _ch) in enumerate(self.recs[e]):
                    for dep in dl:
                        p, c = eff(e, i, dep)
                        if p not in DMAQ:
                            targets[p].add(c)
            mile = {e: {} for e in ENGS}
            for e in ENGS:
                n = 0
                for i in sorted(targets[e]):
                    n += 1
                    mile[e][i] = n
            hw = {e: {} for e in ENGS}
            sched = {p: [] for p in set(PHYS.values())}
            for eng, idx in self.order:
                emit_fn, dl, ch = self.recs[eng][idx]
                waits = []
                for dep in dl:
                    p, c = eff(eng, idx, dep)
                    if p in DMAQ:
                        wch, wn = chpos[(p, c)]
                        key, v = ("c", wch), 16 * wn
                    else:
                        ms = mile[p].get(c)
                        if ms is None:
                            if p == eng:
                                continue
                            raise AssertionError("dep without milestone")
                        key, v = p, ms
                    if v > hw[eng].get(key, 0):
                        waits.append((key, v))
                        hw[eng][key] = v
                if ch is not None:
                    inc = ("c", ch)
                elif idx in mile[eng]:
                    inc = eng
                else:
                    inc = None
                sched[PHYS[eng]].append((eng, idx, waits, inc, emit_fn))
            endw = [(("c", c), 16 * n) for c, n in sorted(
                chtot.items(), key=lambda kv: str(kv[0]))]
            sched["sync"].append(("sync", -1, endw, None, None))
            return sched

        for _round in range(20):
            sched = mk_schedule()
            cnt = {}
            pos = {ph: 0 for ph in sched}
            total = sum(len(v) for v in sched.values())
            done = 0
            stuck = False
            while done < total:
                progress = False
                for ph, recl in sched.items():
                    while pos[ph] < len(recl):
                        eng, idx, waits, inc, _ = recl[pos[ph]]
                        if any(cnt.get(k, 0) < v for k, v in waits):
                            break
                        if inc is not None:
                            cnt[inc] = cnt.get(inc, 0) + \
                                (16 if isinstance(inc, tuple) else 1)
                        pos[ph] += 1
                        done += 1
                        progress = True
                if not progress:
                    stuck = True
                    break
            if not stuck:
                break
            added = 0
            for ph, recl in sched.items():
                if pos[ph] >= len(recl):
                    continue
                eng, idx, waits, _, _ = recl[pos[ph]]
                if idx < 0:
                    continue
                _, dl, _ch = self.recs[eng][idx]
                for dep in dl:
                    if (len(dep) == 3 and dep[2]
                            and (eng, idx, dep[1]) not in noshift):
                        noshift.add((eng, idx, dep[1]))
                        added += 1
            if added == 0:
                msg = ["DEADLOCK not caused by drain shifts; blocked:"]
                for ph, recl in sched.items():
                    if pos[ph] < len(recl):
                        eng, idx, waits, _, _ = recl[pos[ph]]
                        bl = [(k, v, cnt.get(k, 0)) for k, v in waits
                              if cnt.get(k, 0) < v]
                        msg.append(f"  {ph}: {eng}[{idx}] needs {bl}")
                raise AssertionError("\n".join(msg))
        else:
            raise AssertionError("shift revert did not converge")

        self.shift_fall = {}
        nshift = 0
        for e in ENGS:
            for i, (fn, dl, _ch) in enumerate(self.recs[e]):
                for dep in dl:
                    if len(dep) == 3 and dep[2]:
                        if eff(e, i, dep)[1] == dep[1]:
                            ln = (getattr(fn, "__code__", None)
                                  and fn.__code__.co_firstlineno)
                            self.shift_fall[(e, ln)] = \
                                self.shift_fall.get((e, ln), 0) + 1
                        else:
                            nshift += 1
        self.shift_n = nshift

        streams = {p: [] for p in set(PHYS.values())}
        semmap = dict(self.sems)
        for c, s in chan_sems.items():
            semmap[("c", c)] = s

        def mk_thunk(eng, emit_fn, waits, inc):
            def thunk(engobj):
                for k, v in waits:
                    engobj.wait_ge(semmap[k], v)
                inst = emit_fn(engobj)
                if inc is not None:
                    inst.then_inc(semmap[inc],
                                  16 if isinstance(inc, tuple) else 1)
            return thunk

        def mk_wait(waits):
            def thunk(engobj):
                for k, v in waits:
                    engobj.wait_ge(semmap[k], v)
            return thunk

        for ph, recl in sched.items():
            for eng, idx, waits, inc, emit_fn in recl:
                if emit_fn is None:
                    streams[ph].append(mk_wait(waits))
                else:
                    streams[ph].append(mk_thunk(eng, emit_fn, waits, inc))
        return streams


def build_nc():
    nc = bass.Bass("TRN2", target_bir_lowering=False)

    d_x = nc.dram_tensor("x_in", [SEQ_PER_CORE, S, D], F32, kind="ExternalInput").ap()
    d_out = nc.dram_tensor("x_out", [SEQ_PER_CORE, S, D], F32, kind="ExternalOutput").ap()
    dw = {}
    for nm, shp in [
        ("aWq", [D, D]), ("aWk", [D, D]), ("aWv", [D, D]), ("aWd", [D, D]),
        ("aBq", [128, 8]), ("aBk", [128, 8]), ("aBv_row", [1, D]), ("aBd", [128, 8]),
        ("qkvT", [D, 3 * D]), ("qkvB", [128, 24]), ("bBv_row", [1, D]),
        ("outWT", [D, D]), ("outB", [128, 8]),
        ("w1T", [D, HFF]), ("b1", [128, 32]),
        ("w2T", [HFF, D]), ("b2", [128, 8]),
        ("identF", [128, 128]), ("identB", [128, 128]),
        ("cosT", [128, S]), ("sinT", [128, S]),
        ("maskA", [128, 640]),
    ]:
        dt = BF16 if nm in ("cosT", "sinT", "maskA", "identB") else F32
        dw[nm] = nc.dram_tensor(nm, shp, dt, kind="ExternalInput").ap()
    d_w2q = nc.dram_tensor("w2q_scr", [HFF, D], BF16).ap()
    WBF_SHAPES = {"aWq": D, "aWk": D, "aWv": D, "aWd": D,
                  "qkvT": 3 * D, "outWT": D, "w1T": HFF}
    d_wb = {nm: nc.dram_tensor("wb_" + nm, [D, F], BF16).ap()
            for nm, F in WBF_SHAPES.items()}
    d_taps = None
    if DEBUG_TAPS:
        d_taps = nc.dram_tensor("taps", [4, SEQ_PER_CORE, S, D], F32,
                                kind="ExternalOutput").ap()

    ctx = ExitStack()

    def sbuf(name, shape, dt):
        return ctx.enter_context(nc.sbuf_tensor(name, shape, dt))

    def psum(name, shape, dt):
        return ctx.enter_context(nc.psum_tensor(name, shape, dt))

    RES = sbuf("RES", [128, NT * D], F32)
    XLNT = sbuf("XLNT", [128, NC * S], BF16)
    KT = sbuf("KT", [128, NC * S], BF16)
    QH = sbuf("QH", [128, NC * TH], BF16)
    VST = sbuf("VST", [128, NT * D], BF16)
    SHR = sbuf("SHR", [128, 4096], BF16)
    ATO = sbuf("ATO", [128, 1024], BF16)
    SCR = sbuf("SCR", [128, 1024], F32)
    F8S = sbuf("F8S", [128, 2048], FP8)
    F16S = sbuf("F16S", [128, 1024], FP16)
    STT = sbuf("STT", [128, 64], F32)
    ST2 = sbuf("ST2", [128, 8], F32)
    IDF = sbuf("IDF", [128, 128], F32)
    IDB = sbuf("IDB", [128, 128], BF16)
    BIA = {nm: sbuf("s_" + nm, list(dw[nm].shape), F32)
           for nm in ("aBq", "aBk", "aBd", "qkvB", "outB", "b1", "b2")}
    BVR = sbuf("BVR", [33, D], BF16)
    B2QF = sbuf("B2QF", [128, 8], F32)
    B2Q16 = sbuf("B2Q16", [128, 8], FP16)
    ONEB = sbuf("ONEB", [33, 128], BF16)
    MSKA = sbuf("MSKA", [128, 640], BF16)
    COS = sbuf("COS", [128, S], BF16)
    SIN = sbuf("SIN", [128, S], BF16)
    EPS = sbuf("EPS", [128, 1], F32)
    FNC = sbuf("FNC", [128, 4], F32)

    P4A = psum("P4A", [128, 2048], F32)
    PTB = psum("PTB", [128, 1024], BF16)
    PAV = psum("PAV", [128, 512], F32)
    PSC = psum("PSC", [128, 1024], F32)

    sems = {e: ctx.enter_context(nc.semaphore("sm_" + e)) for e in ENGS}
    b = Bld(nc, sems)
    b.dummy = {
        "pe": (lambda e: nc.tensor.ldweights(IDB[:]), (("IDB",),)),
        "act": (lambda e: e.activation(out=FNC[:, 0:1], in_=EPS[:],
                                       func=AF.Identity), (("EPS",),)),
        "dve": (lambda e: e.tensor_copy(out=FNC[:, 1:2], in_=EPS[:]),
                (("EPS",),)),
        "pool": (lambda e: e.tensor_copy(out=FNC[:, 2:3], in_=EPS[:]),
                 (("EPS",),)),
    }

    def pa(i):
        return ("P4A", i)

    def shr(i):
        return SHR[:, 1024 * i:1024 * (i + 1)]

    def SH(i):
        return ("SHR", i)

    def emit_setup():
        b.op("sync", lambda e: e.dma_start(out=IDF[:], in_=dw["identF"]),
             writes=[("IDF",)], chan=("k", "IDF"))
        b.op("sync", lambda e: e.dma_start(out=IDB[:], in_=dw["identB"]),
             writes=[("IDB",)], chan=("k", "IDB"))
        b.op("sync", lambda e: e.dma_start(out=MSKA[:], in_=dw["maskA"]),
             writes=[("MSKA",)], chan=("k", "MSKA"))
        for nm in BIA:
            b.op("sync", lambda e, nm=nm: e.dma_start(out=BIA[nm][:], in_=dw[nm]),
                 writes=[("BIA", nm)], chan=("k", nm))
        b.op("gdma", lambda e: e.dma_start(out=BVR[0:1, 0:D], in_=dw["aBv_row"]),
             writes=[("BVR",)], chan=("k", "BVR"))
        b.op("gdma", lambda e: e.dma_start(out=BVR[32:33, 0:D], in_=dw["bBv_row"]),
             writes=[("BVR",)], chan=("k", "BVR"))
        b.op("dve", lambda e: e.memset(ONEB[:], 1.0), writes=[("ONEB",)])
        b.op("dve", lambda e: e.memset(EPS[:], 1e-5), writes=[("EPS",)])
        b.op("dve", lambda e: e.tensor_copy(out=B2Q16[:], in_=BIA["b2"][:]),
             reads=[("BIA", "b2")], writes=[("B2Q16",)])
        b.op("dve", lambda e: e.tensor_copy(out=B2QF[:], in_=B2Q16[:]),
             reads=[("B2Q16",)], writes=[("B2QF",)])
        b.op("sync", lambda e: e.dma_start(out=COS[:], in_=dw["cosT"]),
             writes=[("COS",)], chan=("k", "COS"))
        b.op("sync", lambda e: e.dma_start(out=SIN[:], in_=dw["sinT"]),
             writes=[("SIN",)], chan=("k", "SIN"))
        for r in range(32):
            i = r % 2
            src = dw["w2T"][128 * r:128 * (r + 1), :]
            b.op("gdma", lambda e, src=src, i=i: e.dma_start(
                out=RES[:, 1024 * i:1024 * (i + 1)], in_=src),
                writes=[("RES", i)], chan=("xl", i))
            f8 = F8S[:, 1024 * i:1024 * (i + 1)]
            b.op("act", lambda e, i=i, f8=f8: e.activation(
                out=f8, in_=RES[:, 1024 * i:1024 * (i + 1)], func=AF.Identity),
                reads=[("RES", i)], writes=[("F8S", i)])
            b.op("act", lambda e, i=i, f8=f8: e.activation(out=shr(i), in_=f8,
                                                           func=AF.Identity),
                 reads=[("F8S", i)], writes=[SH(i)])
            b.op("sync", lambda e, r=r, i=i: e.dma_start(
                out=d_w2q[128 * r:128 * (r + 1), :], in_=shr(i)),
                reads=[SH(i)], writes=[("W2QD",)], chan=("dwq",))
        ci = 0
        for nm, F in WBF_SHAPES.items():
            for rr in range(NC):
                for cc in range(F // 1024):
                    i = ci % 4
                    ci += 1
                    srcc = dw[nm][128 * rr:128 * (rr + 1), 1024 * cc:1024 * (cc + 1)]
                    b.op("gdma", lambda e, srcc=srcc, i=i: e.dma_start(
                        out=RES[:, 1024 * i:1024 * (i + 1)], in_=srcc),
                        writes=[("RES", i)], chan=("xl", i))
                    b.op("act", lambda e, i=i: e.activation(
                        out=shr(i), in_=RES[:, 1024 * i:1024 * (i + 1)],
                        func=AF.Identity),
                        reads=[("RES", i)], writes=[SH(i)])
                    b.op("sync", lambda e, nm=nm, rr=rr, cc=cc, i=i: e.dma_start(
                        out=d_wb[nm][128 * rr:128 * (rr + 1),
                                     1024 * cc:1024 * (cc + 1)], in_=shr(i)),
                        reads=[SH(i)], writes=[("WB", nm)], chan=("dwb", nm))

    def emit_ln():
        for t in range(NT):
            rt = RES[:, D * t:D * (t + 1)]
            b.op("act", lambda e, rt=rt, t=t: e.activation(
                out=shr(t % 2), in_=rt, func=AF.Square,
                accum_out=STT[:, 16 + t:17 + t]),
                reads=[("RES", t)], writes=[SH(t % 2), ("STT", "sq", t)])
            b.op("dve", lambda e, rt=rt, t=t: e.reduce_sum(
                out=STT[:, t:t + 1], in_=rt, axis=AX.X),
                reads=[("RES", t)], writes=[("STT", "sum", t)])
        for t in range(NT):
            b.op("dve", lambda e, t=t: e.tensor_scalar(
                out=STT[:, 32 + t:33 + t], in0=STT[:, t:t + 1],
                scalar1=1.0 / D, scalar2=None, op0=ALU.mult),
                reads=[("STT", "sum", t)], writes=[("STT", "m", t)])
            b.op("dve", lambda e, t=t: e.tensor_scalar(
                out=STT[:, 48 + t:49 + t], in0=STT[:, 32 + t:33 + t],
                scalar1=STT[:, 32 + t:33 + t], scalar2=None, op0=ALU.mult),
                reads=[("STT", "m", t)], writes=[("STT", "v", t)])
            b.op("dve", lambda e, t=t: e.scalar_tensor_tensor(
                out=STT[:, 48 + t:49 + t], in0=STT[:, 16 + t:17 + t],
                scalar=1.0 / D, in1=STT[:, 48 + t:49 + t],
                op0=ALU.mult, op1=ALU.subtract),
                reads=[("STT", "sq", t), ("STT", "v", t)], writes=[("STT", "v", t)])
        for t in range(NT):
            b.op("act", lambda e, t=t: e.activation(
                out=STT[:, 48 + t:49 + t], in_=STT[:, 48 + t:49 + t],
                func=AF.Ln, bias=EPS[:, 0:1]),
                reads=[("STT", "v", t), ("EPS",)], writes=[("STT", "v", t)])
            b.op("act", lambda e, t=t: e.activation(
                out=STT[:, 48 + t:49 + t], in_=STT[:, 48 + t:49 + t],
                func=AF.Exp, scale=-0.5),
                reads=[("STT", "v", t)], writes=[("STT", "v", t)])
        for t in range(NT):
            rt = RES[:, D * t:D * (t + 1)]
            scr2 = SCR[:, 0:1024] if t % 2 == 0 else SCR[:, 0:1024]
            b.op("dve", lambda e, rt=rt, t=t: e.tensor_scalar(
                out=SCR[:], in0=rt, scalar1=STT[:, 32 + t:33 + t],
                scalar2=STT[:, 48 + t:49 + t], op0=ALU.subtract, op1=ALU.mult),
                reads=[("RES", t), ("STT", "m", t), ("STT", "v", t)],
                writes=[("SCR",)])
            off = 0 if t % 2 == 0 else 1024
            pks = [pa(2 * (t % 2)), pa(2 * (t % 2) + 1)]
            for j in range(NC):
                b.op("pe", lambda e, j=j, off=off: nc.tensor.transpose(
                    out=P4A[:, off + 128 * j:off + 128 * (j + 1)],
                    in_=SCR[:, 128 * j:128 * (j + 1)], identity=IDF[:]),
                    reads=[("SCR",), ("IDF",)], writes=[pks[j // 4]])
            dst = XLNT[:].rearrange("p (c x) -> p c x", c=NC)[:, :, 128 * t:128 * (t + 1)]
            b.op("act", lambda e, dst=dst, off=off: e.activation(
                out=dst, in_=P4A[:, off:off + 1024], func=AF.Identity),
                reads=pks, writes=[("XLNT", t)])

    def emit_proj_fm(W_ap, f_list, dst_sb, dst_stride, t0, tn, bias_nm, dst_tag):
        nmm = tn // 512
        tkeys = [("XLNT", t) for t in range(t0 // 128, (t0 + tn) // 128)]
        for i, (fc_src, fc_dst) in enumerate(f_list):
            si = i % 2
            src = W_ap[:, 128 * fc_src:128 * (fc_src + 1)].rearrange(
                "(j p) c -> p j c", p=128)
            b.op("gdma", lambda e, src=src, si=si: e.dma_start(
                out=shr(si)[:].rearrange("p (j c) -> p j c", j=NC), in_=src),
                writes=[SH(si)], chan=("shr", si))
            for d in range(NC):
                rhs = XLNT[:, S * d + t0:S * d + t0 + tn]
                for m in range(nmm):
                    b.op("pe", lambda e, si=si, d=d, m=m, rhs=rhs,
                         st_=(d == 0), sp_=(d == NC - 1): nc.tensor.matmul(
                        out=P4A[:, 512 * m:512 * (m + 1)],
                        lhsT=shr(si)[:, 128 * d:128 * (d + 1)],
                        rhs=rhs[:, 512 * m:512 * (m + 1)],
                        start=st_, stop=sp_),
                        reads=[SH(si)] + tkeys, writes=[pa(m)])
            for m in range(nmm):
                dst = dst_sb[:, dst_stride * fc_dst + 512 * m:
                             dst_stride * fc_dst + 512 * (m + 1)]
                bia = BIA[bias_nm][:, fc_src:fc_src + 1]
                if dst_tag == "QH":
                    wkeys = [("QH", fc_dst, q) for q in range(4 * m, 4 * m + 4)]
                else:
                    wkeys = [(dst_tag, fc_dst)]
                b.op("act", lambda e, dst=dst, m=m, bia=bia: e.activation(
                    out=dst, in_=P4A[:, 512 * m:512 * (m + 1)], func=AF.Identity,
                    bias=bia),
                    reads=[pa(m), ("BIA", bias_nm)], writes=wkeys)

    def emit_vproj(Wv_ap, bvr_off):
        for tg in range(NT // 2):
            pk = [[("PSC",)], [pa(0), pa(1)]]
            pss = [PSC, P4A]
            for d in range(NC):
                si = 2 + d % 2
                src = Wv_ap[128 * d:128 * (d + 1), :]
                b.op("gdma", lambda e, src=src, si=si: e.dma_start(out=shr(si), in_=src),
                     writes=[SH(si)], chan=("shr", si))
                for ti in range(2):
                    t = 2 * tg + ti
                    lhs = XLNT[:].rearrange("p (c x) -> p c x", c=NC)[
                        :, d, 128 * t:128 * (t + 1)]
                    ps = pss[ti]
                    for m in range(2):
                        b.op("pe", lambda e, lhs=lhs, si=si, ps=ps, m=m, d=d:
                             nc.tensor.matmul(
                                 out=ps[:, 512 * m:512 * (m + 1)], lhsT=lhs,
                                 rhs=shr(si)[:, 512 * m:512 * (m + 1)],
                                 start=(d == 0), stop=False),
                             reads=[SH(si), ("XLNT", 2 * tg + ti)],
                             writes=pk[ti])
            for ti in range(2):
                ps = pss[ti]
                for m in range(2):
                    b.op("pe", lambda e, ps=ps, m=m: nc.tensor.matmul(
                        out=ps[:, 512 * m:512 * (m + 1)],
                        lhsT=ONEB[bvr_off:bvr_off + 1, :],
                        rhs=BVR[bvr_off:bvr_off + 1, 512 * m:512 * (m + 1)],
                        start=False, stop=True),
                        reads=[("ONEB",), ("BVR",)], writes=pk[ti])
            for ti in range(2):
                t = 2 * tg + ti
                ps = pss[ti]
                dst = VST[:, 1024 * t:1024 * (t + 1)]
                b.op("act", lambda e, dst=dst, ps=ps: e.activation(
                    out=dst, in_=ps[:, 0:1024], func=AF.Identity),
                    reads=pk[ti], writes=[("VST", t)])

    def emit_rope(buf_sb, col0, key, cs_off, par=0):
        keys = key if isinstance(key, list) else [key]
        x = buf_sb[:, col0:col0 + 1024]
        s0, s1 = (2, 3) if par % 2 == 0 else (0, 1)
        swx, a_ = shr(s0), shr(s1)
        cbuf = COS[:, cs_off:cs_off + 1024]
        sbuf_ = SIN[:, cs_off:cs_off + 1024]
        for blk in range(4):
            srcp = 32 * (blk + 1) if blk % 2 == 0 else 32 * (blk - 1)
            b.op("pool", lambda e, srcp=srcp, blk=blk, x=x, swx=swx: e.tensor_copy(
                out=swx[32 * blk:32 * blk + 32, :], in_=x[srcp:srcp + 32, :]),
                reads=keys, writes=[SH(s0)])
        b.op("dve", lambda e, x=x, a_=a_, cbuf=cbuf: e.tensor_mul(out=a_, in0=x,
                                                                  in1=cbuf),
             reads=keys + [("COS",)], writes=[SH(s1)])
        b.op("dve", lambda e, swx=swx, sbuf_=sbuf_: e.tensor_mul(out=swx, in0=swx,
                                                                 in1=sbuf_),
             reads=[SH(s0), ("SIN",)], writes=[SH(s0)])
        b.op("dve", lambda e, x=x, a_=a_, swx=swx: e.tensor_add(out=x, in0=a_,
                                                                in1=swx),
             reads=[SH(s1), SH(s0)], writes=keys)

    SC_R = [(0, 1), (2, 3), (6, 7)]

    def emit_mixer(hh, banded):
        rot = [0]

        def sc_region(r):
            if r == 0:
                return P4A[:, 0:1024], [pa(0), pa(1)]
            if r == 1:
                return P4A[:, 1024:2048], [pa(2), pa(3)]
            return PSC[:, 0:1024], [("PSC",)]

        def geom(qt):
            qtg = 8 * hh + qt
            if banded:
                k0 = max(0, 128 * qtg - WINDOW)
                k1 = min(S, 128 * qtg + 384)
            else:
                k0, k1 = 0, S
            return k0, k1 - k0

        items = []
        for qt in range(8):
            k0, w = geom(qt)
            nh = (w + 1023) // 1024
            for h in range(NH):
                for m in range(nh):
                    kk0 = k0 + 1024 * m
                    ww = min(1024, k0 + w - kk0)
                    items.append((qt, h, m, kk0, ww, m == nh - 1))
        nit = len(items)
        regs = [None] * nit

        def stage_scores(n):
            qt, h, m, kk0, ww, last = items[n]
            r = rot[0] % 3
            rot[0] += 1
            regs[n] = r
            ps, sk = sc_region(r)
            ch, sub = h // 2, h % 2
            lhsq = QH[:, TH * ch + 128 * qt:TH * ch + 128 * (qt + 1)][
                64 * sub:64 * sub + 64, :]
            for m0 in range(0, ww, 512):
                m1 = min(m0 + 512, ww)
                rhs = KT[:, S * ch + kk0 + m0:S * ch + kk0 + m1][
                    64 * sub:64 * sub + 64, :]
                b.op("pe", lambda e, lhsq=lhsq, rhs=rhs, m0=m0, m1=m1, ps=ps:
                     nc.tensor.matmul(out=ps[:, m0:m1], lhsT=lhsq, rhs=rhs,
                                      start=True, stop=True),
                     reads=[("KT", ch), ("QH", ch, qt)], writes=sk)
            if banded:
                qtg = 8 * hh + qt
                moff = kk0 - (128 * qtg - WINDOW)
                b.op("dve", lambda e, ww=ww, ps=ps, moff=moff: e.tensor_add(
                    out=ps[:, 0:ww], in0=ps[:, 0:ww],
                    in1=MSKA[:, moff:moff + ww]),
                    reads=sk + [("MSKA",)], writes=sk)

        def stage_max(n):
            qt, h, m, kk0, ww, last = items[n]
            ps, sk = sc_region(regs[n])
            b.op("dve", lambda e, ww=ww, ps=ps, m=m: e.reduce_max(
                out=ST2[:, m:m + 1], in_=ps[:, 0:ww], axis=AX.X),
                reads=sk, writes=[("ST2", "mx", m)])

        def stage_comb(n):
            qt, h, m, kk0, ww, last = items[n]
            if m == 0:
                b.op("dve", lambda e: e.tensor_scalar(
                    out=ST2[:, 2:3], in0=ST2[:, 0:1], scalar1=-0.125,
                    scalar2=None, op0=ALU.mult),
                    reads=[("ST2", "mx", 0)], writes=[("ST2", "mb")])
            else:
                b.op("dve", lambda e: e.tensor_max(
                    out=ST2[:, 0:1], in0=ST2[:, 0:1], in1=ST2[:, 1:2]),
                    reads=[("ST2", "mx", 0), ("ST2", "mx", 1)],
                    writes=[("ST2", "mx", 0)])
                b.op("dve", lambda e: e.tensor_scalar(
                    out=ST2[:, 2:3], in0=ST2[:, 0:1], scalar1=-0.125,
                    scalar2=None, op0=ALU.mult),
                    reads=[("ST2", "mx", 0)], writes=[("ST2", "mb")])

        def stage_exp(n):
            qt, h, m, kk0, ww, last = items[n]
            ps, sk = sc_region(regs[n])
            es = n % 2
            ek = SH(es)
            ebuf = shr(es)
            zc = 4 + 2 * gpar[n] + m
            b.op("act", lambda e, ww=ww, ps=ps, ebuf=ebuf, zc=zc: e.activation(
                out=ebuf[:, 0:ww], in_=ps[:, 0:ww], func=AF.Exp,
                scale=0.125, bias=ST2[:, 2:3], accum_out=ST2[:, zc:zc + 1]),
                reads=sk + [("ST2", "mb")], writes=[ek, ("ST2", "z", gpar[n], m)])

        def stage_tr(n):
            qt, h, m, kk0, ww, last = items[n]
            es = n % 2
            ek = SH(es)
            ebuf = shr(es)
            nkb = ww // 128
            for j in range(nkb):
                b.op("pe", lambda e, j=j, ebuf=ebuf: nc.tensor.transpose(
                    out=PTB[:, 128 * j:128 * (j + 1)],
                    in_=ebuf[:, 128 * j:128 * (j + 1)], identity=IDB[:]),
                    reads=[ek, ("IDB",)], writes=[("PTB",)])
            pt = 2 + n % 2
            ptk = SH(pt)
            ptbuf = shr(pt)
            if n % 2:
                b.op("act", lambda e, nkb=nkb, ptbuf=ptbuf: e.activation(
                    out=ptbuf[:, 0:128 * nkb], in_=PTB[:, 0:128 * nkb],
                    func=AF.Identity),
                    reads=[("PTB",)], writes=[ptk])
            else:
                b.op("dve", lambda e, nkb=nkb, ptbuf=ptbuf: e.tensor_copy(
                    out=ptbuf[:, 0:128 * nkb], in_=PTB[:, 0:128 * nkb]),
                    reads=[("PTB",)], writes=[ptk])

        def stage_av(n):
            qt, h, m, kk0, ww, last = items[n]
            nkb = ww // 128
            pt = 2 + n % 2
            ptk = SH(pt)
            ptbuf = shr(pt)
            for i in range(nkb):
                kt = kk0 // 128 + i
                lhs = ptbuf[:, 128 * i:128 * (i + 1)]
                rhs = VST[:, 1024 * kt + 64 * h:1024 * kt + 64 * h + 64]
                b.op("pe", lambda e, lhs=lhs, rhs=rhs,
                     st_=(m == 0 and i == 0),
                     sp_=(last and i == nkb - 1): nc.tensor.matmul(
                    out=PAV[:, 0:64], lhsT=lhs, rhs=rhs, start=st_, stop=sp_),
                    reads=[ptk, ("VST", kt)], writes=[("PAV",)])

        def stage_norm(n):
            qt, h, m, kk0, ww, last = items[n]
            gp = gpar[n]
            z0 = 4 + 2 * gp
            if m == 1:
                b.op("dve", lambda e, z0=z0: e.tensor_add(
                    out=ST2[:, z0:z0 + 1], in0=ST2[:, z0:z0 + 1],
                    in1=ST2[:, z0 + 1:z0 + 2]),
                    reads=[("ST2", "z", gp, 0), ("ST2", "z", gp, 1)],
                    writes=[("ST2", "z", gp, 0)])
            b.op("dve", lambda e, z0=z0: e.reciprocal(out=ST2[:, 3:4],
                                                      in_=ST2[:, z0:z0 + 1]),
                 reads=[("ST2", "z", gp, 0)], writes=[("ST2", "rz")])
            b.op("dve", lambda e, h=h: e.tensor_scalar(
                out=ATO[:, 64 * h:64 * (h + 1)], in0=PAV[:, 0:64],
                scalar1=ST2[:, 3:4], scalar2=None, op0=ALU.mult),
                reads=[("PAV",), ("ST2", "rz")], writes=[("ATO",)])

        def stage_ato(qt):
            for j in range(NC):
                b.op("pe", lambda e, j=j: nc.tensor.transpose(
                    out=PTB[:, 128 * j:128 * (j + 1)],
                    in_=ATO[:, 128 * j:128 * (j + 1)], identity=IDB[:]),
                    reads=[("ATO",), ("IDB",)], writes=[("PTB",)])
            dst = QH[:].rearrange("p (c x) -> p c x", c=NC)[
                :, :, 128 * qt:128 * (qt + 1)]
            b.op("act", lambda e, dst=dst: e.activation(
                out=dst, in_=PTB[:], func=AF.Identity),
                reads=[("PTB",)],
                writes=[("QH", j, qt) for j in range(NC)])

        groups = []
        gpar = {}
        i = 0
        while i < nit:
            j = i
            while not items[j][5]:
                j += 1
            groups.append(list(range(i, j + 1)))
            for n_ in range(i, j + 1):
                gpar[n_] = (len(groups) - 1) % 2
            i = j + 1
        for n in groups[0]:
            stage_scores(n)
        avq = []
        normq = []
        for gi, grp in enumerate(groups):
            for n in grp:
                stage_max(n)
            for n in grp:
                stage_comb(n)
            nxt = groups[gi + 1] if gi + 1 < len(groups) else []
            for k, n in enumerate(grp):
                if k < len(nxt):
                    stage_scores(nxt[k])
                stage_exp(n)
                stage_tr(n)
                if avq:
                    stage_av(avq.pop(0))
                avq.append(n)
                while normq and items[normq[0]][5] and normq[0] not in avq:
                    nl = normq.pop(0)
                    stage_norm(nl)
                    qtl, hl = items[nl][0], items[nl][1]
                    if hl == NH - 1:
                        stage_ato(qtl)
            for k in range(len(grp), len(nxt)):
                stage_scores(nxt[k])
            normq.append(grp[-1])
        while avq:
            stage_av(avq.pop(0))
        while normq:
            nl = normq.pop(0)
            stage_norm(nl)
            qtl, hl = items[nl][0], items[nl][1]
            if hl == NH - 1:
                stage_ato(qtl)

    def emit_outproj(hh, W_ap, bias_nm):
        for g in range(NC):
            si = 2 + g % 2
            src = W_ap[:, 128 * g:128 * (g + 1)].rearrange("(j p) c -> p j c", p=128)
            b.op("gdma", lambda e, src=src, si=si: e.dma_start(
                out=shr(si)[:].rearrange("p (j c) -> p j c", j=NC), in_=src),
                writes=[SH(si)], chan=("shr", si))
            for d in range(NC):
                rhs = QH[:, TH * d:TH * (d + 1)]
                for m in range(2):
                    b.op("pe", lambda e, si=si, d=d, m=m, rhs=rhs: nc.tensor.matmul(
                        out=PSC[:, 512 * m:512 * (m + 1)],
                        lhsT=shr(si)[:, 128 * d:128 * (d + 1)],
                        rhs=rhs[:, 512 * m:512 * (m + 1)],
                        start=(d == 0), stop=(d == NC - 1)),
                        reads=[SH(si)] + [("QH", d, q) for q in range(8)],
                        writes=[("PSC",)])
            bia = BIA[bias_nm][:, g:g + 1]
            b.op("act", lambda e, bia=bia: e.activation(
                out=SCR[:], in_=PSC[:], func=AF.Identity, bias=bia),
                reads=[("PSC",), ("BIA", bias_nm)], writes=[("SCR",)])
            off = 0 if g % 2 == 0 else 1024
            opk = [pa(2 * (g % 2)), pa(2 * (g % 2) + 1)]
            for j in range(NC):
                b.op("pe", lambda e, j=j, off=off: nc.tensor.transpose(
                    out=P4A[:, off + 128 * j:off + 128 * (j + 1)],
                    in_=SCR[:, 128 * j:128 * (j + 1)], identity=IDF[:]),
                    reads=[("SCR",), ("IDF",)], writes=opk)
            rview = RES[:].rearrange("p (t x) -> p t x", t=NT)[
                :, 8 * hh:8 * hh + 8, 128 * g:128 * (g + 1)]
            pview = P4A[:, off:off + 1024].rearrange("p (t x) -> p t x", t=8)
            b.op("dve", lambda e, rview=rview, pview=pview: e.tensor_add(
                out=rview, in0=rview, in1=pview),
                reads=opk + [("RES", 8 * hh + j) for j in range(8)],
                writes=[("RES", 8 * hh + j) for j in range(8)])

    def hq_slot(hc):
        if hc < 16:
            return KT, hc, ("KT", hc)
        return VST, hc - 16, ("VST", "hq", hc - 16)

    def emit_ffn():
        for tc in range(2):
            t0 = TH * tc
            tkeys = [("XLNT", t) for t in range(8 * tc, 8 * tc + 8)]
            for hc in range(32):
                si = 2 + hc % 2
                src = d_wb["w1T"][:, 128 * hc:128 * (hc + 1)].rearrange(
                    "(j p) c -> p j c", p=128)
                b.op("gdma", lambda e, src=src, si=si: e.dma_start(
                    out=shr(si)[:].rearrange("p (j c) -> p j c", j=NC), in_=src),
                    writes=[SH(si)], chan=("shr", si))
                pk = [("PSC",)] if hc % 2 == 0 else [pa(0), pa(1)]
                ps = PSC if hc % 2 == 0 else P4A
                for d in range(NC):
                    rhs = XLNT[:, S * d + t0:S * d + t0 + TH]
                    for m in range(2):
                        b.op("pe", lambda e, si=si, d=d, m=m, rhs=rhs, ps=ps:
                             nc.tensor.matmul(
                                 out=ps[:, 512 * m:512 * (m + 1)],
                                 lhsT=shr(si)[:, 128 * d:128 * (d + 1)],
                                 rhs=rhs[:, 512 * m:512 * (m + 1)],
                                 start=(d == 0), stop=(d == NC - 1)),
                             reads=[SH(si)] + tkeys, writes=pk)
                f8 = F8S[:, 1024 * (hc % 2):1024 * (hc % 2 + 1)]
                fk = ("F8S", hc % 2)
                b.op("act", lambda e, ps=ps, f8=f8, hc=hc: e.activation(
                    out=f8, in_=ps[:, 0:1024], func=AF.Relu,
                    bias=BIA["b1"][:, hc:hc + 1]),
                    reads=pk + [("BIA", "b1")], writes=[fk])
                hq, hqc, hqk = hq_slot(hc)
                dst = hq[:, 1024 * hqc:1024 * (hqc + 1)]
                b.op("act", lambda e, f8=f8, dst=dst: e.activation(
                    out=dst, in_=f8, func=AF.Identity),
                    reads=[fk], writes=[hqk])
            for g in range(NC):
                for half in range(2):
                    src = d_w2q[2048 * half:2048 * (half + 1),
                                128 * g:128 * (g + 1)].rearrange(
                        "(j p) c -> p j c", p=128)
                    dsts = SHR[:, 2048 * half:2048 * (half + 1)]
                    hkeys = [SH(2 * half), SH(2 * half + 1)]
                    b.op("gdma", lambda e, src=src, dsts=dsts: e.dma_start(
                        out=dsts[:].rearrange("p (j c) -> p j c", j=16), in_=src),
                        reads=[("W2QD",)], writes=hkeys, chan=("shr", 2 * half))
                pk = [("PSC",)] if g % 2 == 0 else [pa(0), pa(1)]
                ps = PSC if g % 2 == 0 else P4A
                for hc in range(32):
                    half = hc // 16
                    lt = SHR[:, 2048 * half + 128 * (hc % 16):
                             2048 * half + 128 * (hc % 16) + 128]
                    hq, hqc, hqk = hq_slot(hc)
                    rhs = hq[:, 1024 * hqc:1024 * (hqc + 1)]
                    for m in range(2):
                        b.op("pe", lambda e, lt=lt, m=m, rhs=rhs, ps=ps, hc=hc:
                             nc.tensor.matmul(
                                 out=ps[:, 512 * m:512 * (m + 1)], lhsT=lt,
                                 rhs=rhs[:, 512 * m:512 * (m + 1)],
                                 start=(hc == 0), stop=(hc == 31)),
                             reads=[SH(2 * half), SH(2 * half + 1), hqk],
                             writes=pk)
                b.op("act", lambda e, ps=ps, g=g: e.activation(
                    out=F16S[:], in_=ps[:, 0:1024], func=AF.Identity,
                    bias=B2QF[:, g:g + 1]),
                    reads=pk + [("B2QF",)], writes=[("F16S",)])
                b.op("act", lambda e: e.activation(out=SCR[:], in_=F16S[:],
                                                   func=AF.Identity),
                     reads=[("F16S",)], writes=[("SCR",)])
                off = 1024 if g % 2 == 0 else 0
                opk = [pa(2 * (1 - g % 2)), pa(2 * (1 - g % 2) + 1)]
                for j in range(NC):
                    b.op("pe", lambda e, j=j, off=off: nc.tensor.transpose(
                        out=P4A[:, off + 128 * j:off + 128 * (j + 1)],
                        in_=SCR[:, 128 * j:128 * (j + 1)], identity=IDF[:]),
                        reads=[("SCR",), ("IDF",)], writes=opk)
                rview = RES[:].rearrange("p (t x) -> p t x", t=NT)[
                    :, 8 * tc:8 * tc + 8, 128 * g:128 * (g + 1)]
                pview = P4A[:, off:off + 1024].rearrange("p (t x) -> p t x", t=8)
                b.op("dve", lambda e, rview=rview, pview=pview: e.tensor_add(
                    out=rview, in0=rview, in1=pview),
                    reads=opk + [("RES", 8 * tc + j) for j in range(8)],
                    writes=[("RES", 8 * tc + j) for j in range(8)])

    def emit_attn(which):
        banded = which == "a"
        if banded:
            kW, kBias = d_wb["aWk"], "aBk"
            qW, qBias = d_wb["aWq"], "aBq"
            vW, bvro = d_wb["aWv"], 0
            oW, oBias = d_wb["aWd"], "aBd"
            kf = [(c, c) for c in range(NC)]
            qf = [(c, c) for c in range(NC)]
        else:
            kW, kBias = d_wb["qkvT"], "qkvB"
            qW, qBias = d_wb["qkvT"], "qkvB"
            vW, bvro = d_wb["qkvT"][:, 2 * D:3 * D], 32
            oW, oBias = d_wb["outWT"], "outB"
            kf = [(8 + c, c) for c in range(NC)]
            qf = [(c, c) for c in range(NC)]
        emit_proj_fm(kW, kf, KT, S, 0, S, kBias, "KT")
        if not banded:
            for ch in range(NC):
                for half in range(2):
                    emit_rope(KT, S * ch + TH * half, ("KT", ch), TH * half,
                              par=2 * ch + half)
        emit_vproj(vW, bvro)
        for hh in range(2):
            emit_proj_fm(qW, qf, QH, TH, TH * hh, TH, qBias, "QH")
            if not banded:
                for ch in range(NC):
                    emit_rope(QH, TH * ch,
                              [("QH", ch, q) for q in range(8)], TH * hh, par=ch)
            emit_mixer(hh, banded)
            emit_outproj(hh, oW, oBias)

    emit_setup()
    for s in range(SEQ_PER_CORE):
        for t in range(NT):
            b.op("sync", lambda e, s=s, t=t: e.dma_start(
                out=RES[:, D * t:D * (t + 1)], in_=d_x[s, 128 * t:128 * (t + 1), :]),
                writes=[("RES", t)], chan=("xl", t))
        for li, sub in enumerate(("a", "f", "b", "f")):
            emit_ln()
            if sub == "f":
                emit_ffn()
            else:
                emit_attn(sub)
            if DEBUG_TAPS:
                for t in range(NT):
                    b.op("sync", lambda e, s=s, t=t, li=li: e.dma_start(
                        out=d_taps[li, s, 128 * t:128 * (t + 1), :],
                        in_=RES[:, D * t:D * (t + 1)]),
                        reads=[("RES", t)], chan=("xl", t))
        for t in range(NT):
            b.op("sync", lambda e, s=s, t=t: e.dma_start(
                out=d_out[s, 128 * t:128 * (t + 1), :],
                in_=RES[:, D * t:D * (t + 1)]),
                reads=[("RES", t)], chan=("xl", t))
    chan_sems = {c: ctx.enter_context(nc.semaphore("c%d" % i))
                 for i, c in enumerate(sorted(b.chans, key=str))}
    streams = b.finalize(chan_sems)

    with nc.Block() as block:
        @block.sync
        def _(e):
            for th in streams["sync"]:
                th(e)

        @block.gpsimd
        def _(e):
            for th in streams["gpsimd"]:
                th(e)

        @block.vector
        def _(e):
            for th in streams["vector"]:
                th(e)

        @block.scalar
        def _(e):
            for th in streams["scalar"]:
                th(e)

        @block.tensor
        def _(e):
            for th in streams["tensor"]:
                th(e)

    ctx.close()
    return nc


def _host_inputs(inputs):
    f32 = np.float32
    g = {k: np.asarray(v) for k, v in inputs.items()}

    p = np.arange(2 * D)
    for h in range(2 * NH):
        blk = np.arange(HD) + h * HD
        p[h * HD:(h + 1) * HD] = np.concatenate([blk[0::2], blk[1::2]])
    qkvT = np.ascontiguousarray(g["L0_attn_b_qkv_w"].T.astype(f32))
    qkvT = np.concatenate([qkvT[:, :2 * D][:, p], qkvT[:, 2 * D:]], axis=1)
    qkvb = g["L0_attn_b_qkv_b"].astype(f32)
    qkvb = np.concatenate([qkvb[:2 * D][p], qkvb[2 * D:]])

    def brect(v, cols):
        return np.ascontiguousarray(v.astype(f32).reshape(cols, 128).T)

    inv = 1.0 / 10000.0 ** (np.arange(0, HD, 2, dtype=np.float64) / HD)
    th = np.arange(S, dtype=np.float64)[:, None] * inv[None, :]
    cos = np.cos(th).T.astype(f32)
    sin = np.sin(th).T.astype(f32)
    cosT = np.tile(cos, (4, 1)).astype(ml_dtypes.bfloat16)
    sinT = np.concatenate([-sin, sin, -sin, sin], axis=0).astype(ml_dtypes.bfloat16)

    pp = np.arange(128)[:, None]
    cc = np.arange(640)[None, :]
    maskA = np.where((cc >= pp) & (cc <= pp + 2 * WINDOW), 0.0,
                     -30000.0).astype(ml_dtypes.bfloat16)

    w = {
        "aWq": np.ascontiguousarray(g["L0_attn_a_Wq"].astype(f32)),
        "aWk": np.ascontiguousarray(g["L0_attn_a_Wk"].astype(f32)),
        "aWv": np.ascontiguousarray(g["L0_attn_a_Wv"].astype(f32)),
        "aWd": np.ascontiguousarray(g["L0_attn_a_Wd"].astype(f32)),
        "aBq": brect(g["L0_attn_a_bq"], 8),
        "aBk": brect(g["L0_attn_a_bk"], 8),
        "aBv_row": np.ascontiguousarray(g["L0_attn_a_bv"].astype(f32)[None, :]),
        "aBd": brect(g["L0_attn_a_bd"], 8),
        "qkvT": np.ascontiguousarray(qkvT),
        "qkvB": brect(qkvb, 24),
        "bBv_row": np.ascontiguousarray(qkvb[2 * D:][None, :]),
        "outWT": np.ascontiguousarray(g["L0_attn_b_out_w"].T.astype(f32)),
        "outB": brect(g["L0_attn_b_out_b"], 8),
        "w1T": np.ascontiguousarray(g["L0_ffn_w1"].T.astype(f32)),
        "b1": brect(g["L0_ffn_b1"], 32),
        "w2T": np.ascontiguousarray(g["L0_ffn_w2"].T.astype(f32)),
        "b2": brect(g["L0_ffn_b2"], 8),
        "identF": np.eye(128, dtype=f32),
        "identB": np.eye(128, dtype=f32).astype(ml_dtypes.bfloat16),
        "cosT": np.ascontiguousarray(cosT),
        "sinT": np.ascontiguousarray(sinT),
        "maskA": np.ascontiguousarray(maskA),
    }
    x = np.asarray(g["L0_attn_a_x"], dtype=f32)
    return x, w


def kernel(**inputs):
    x, w = _host_inputs(inputs)
    nc = build_nc()
    in_maps = []
    for core in range(NCORES):
        im = {"x_in": np.ascontiguousarray(
            x[SEQ_PER_CORE * core:SEQ_PER_CORE * (core + 1)])}
        im.update(w)
        in_maps.append(im)
    res = run_bass_kernel_spmd(nc, in_maps, core_ids=list(range(NCORES)))
    out = np.concatenate([np.asarray(r["x_out"]) for r in res.results], axis=0)
    if DEBUG_TAPS:
        kernel._taps = np.stack([np.asarray(r["taps"]) for r in res.results])
    return np.ascontiguousarray(out.astype(np.float32))


# version 25 (final)
# speedup vs baseline: 1.0170x; 1.0170x over previous
import numpy as np
import ml_dtypes
from contextlib import ExitStack

import concourse.bass as bass
import concourse.mybir as mybir
from concourse.bass_utils import run_bass_kernel_spmd

F32 = mybir.dt.float32
BF16 = mybir.dt.bfloat16
FP16 = mybir.dt.float16
FP8 = mybir.dt.float8e4
AF = mybir.ActivationFunctionType
ALU = mybir.AluOpType
AX = mybir.AxisListType

NCORES = 8
B, S, D = 16, 2048, 1024
NH, HD = 16, 64
HFF = 4096
WINDOW = 256
SEQ_PER_CORE = B // NCORES
NT = S // 128
NC = D // 128
TH = 1024

ENGS = ("pe", "act", "dve", "sync", "gdma", "pool")
PHYS = {"pe": "tensor", "act": "scalar", "dve": "vector",
        "sync": "sync", "gdma": "gpsimd", "pool": "gpsimd"}
DMAQ = ("sync", "gdma")

DEBUG_TAPS = False
DRAIN_GUARD = False


class Bld:

    def __init__(self, nc, sems):
        self.nc = nc
        self.recs = {e: [] for e in ENGS}
        self.order = []
        self.cnt = {e: 0 for e in ENGS}
        self.wr = {}
        self.rd = {}
        self.sems = sems
        self.final_deps = []
        self.dummy = {}
        self.ndummy = {e: 0 for e in ENGS}
        self.chans = {}

    def op(self, eng, emit_fn, reads=(), writes=(), chan=None):
        assert (chan is not None) == (eng in DMAQ), (eng, chan)
        if chan is not None:
            chan = (eng,) + chan
        deps = {}
        def add(p, c, raw):
            e = deps.setdefault(p, [-1, -1])
            k = 0 if raw else 1
            e[k] = max(e[k], c)
        for r in reads:
            w = self.wr.get(r)
            if w is not None:
                add(w[0], w[1], True)
        for w_ in writes:
            w = self.wr.get(w_)
            if w is not None:
                add(w[0], w[1], True)
            for re_, c in self.rd.get(w_, {}).items():
                add(re_, c, False)
        for p, (craw, cwar) in deps.items():
            if (DRAIN_GUARD and p != eng and p not in DMAQ and craw >= cwar
                    and craw == self.cnt[p] - 1 and p in self.dummy):
                self.ndummy[p] += 1
                dfn, drd = self.dummy[p]
                self.op(p, dfn, reads=drd, writes=[("FNC", p)])
        myc = self.cnt[eng]
        dl = []
        for p, (craw, cwar) in deps.items():
            c = max(craw, cwar)
            if p == eng:
                if eng != "pe":
                    dl.append((p, c))
            else:
                if DRAIN_GUARD and p not in DMAQ and craw >= cwar:
                    dl.append((p, craw, True))
                else:
                    dl.append((p, c))
        self.recs[eng].append((emit_fn, dl, chan))
        self.order.append((eng, myc))
        self.cnt[eng] = myc + 1
        if chan is not None:
            self.chans[chan] = self.chans.get(chan, 0) + 1
        for r in reads:
            self.rd.setdefault(r, {})[eng] = myc
        for w_ in writes:
            self.wr[w_] = (eng, myc)
            self.rd[w_] = {}

    def raw_wait(self, eng, prod, idx=None):
        if idx is None:
            idx = self.cnt[prod] - 1
        self.final_deps.append((eng, prod, idx))

    def finalize(self, chan_sems):
        ncnt = {e: len(self.recs[e]) for e in ENGS}
        chpos = {}
        chtot = {c: 0 for c in self.chans}
        for e, i in self.order:
            ch = self.recs[e][i][2]
            if ch is not None:
                chtot[ch] += 1
                chpos[(e, i)] = (ch, chtot[ch])
        noshift = set()
        emipos = {}
        for n, (e, i) in enumerate(self.order):
            emipos[(e, i)] = n

        def eff(eng, idx, dep):
            p = dep[0]
            if (len(dep) == 3 and dep[2] and dep[1] + 1 < ncnt[p]
                    and emipos[(p, dep[1] + 1)] < emipos[(eng, idx)]
                    and (eng, idx, dep[1]) not in noshift):
                return (p, dep[1] + 1)
            return (p, dep[1])

        def mk_schedule():
            targets = {e: set() for e in ENGS}
            for e in ENGS:
                for i, (_, dl, _ch) in enumerate(self.recs[e]):
                    for dep in dl:
                        p, c = eff(e, i, dep)
                        if p not in DMAQ:
                            targets[p].add(c)
            mile = {e: {} for e in ENGS}
            for e in ENGS:
                n = 0
                for i in sorted(targets[e]):
                    n += 1
                    mile[e][i] = n
            hw = {e: {} for e in ENGS}
            sched = {p: [] for p in set(PHYS.values())}
            for eng, idx in self.order:
                emit_fn, dl, ch = self.recs[eng][idx]
                waits = []
                for dep in dl:
                    p, c = eff(eng, idx, dep)
                    if p in DMAQ:
                        wch, wn = chpos[(p, c)]
                        key, v = ("c", wch), 16 * wn
                    else:
                        ms = mile[p].get(c)
                        if ms is None:
                            if p == eng:
                                continue
                            raise AssertionError("dep without milestone")
                        key, v = p, ms
                    if v > hw[eng].get(key, 0):
                        waits.append((key, v))
                        hw[eng][key] = v
                if ch is not None:
                    inc = ("c", ch)
                elif idx in mile[eng]:
                    inc = eng
                else:
                    inc = None
                sched[PHYS[eng]].append((eng, idx, waits, inc, emit_fn))
            endw = [(("c", c), 16 * n) for c, n in sorted(
                chtot.items(), key=lambda kv: str(kv[0]))]
            sched["sync"].append(("sync", -1, endw, None, None))
            return sched

        for _round in range(20):
            sched = mk_schedule()
            cnt = {}
            pos = {ph: 0 for ph in sched}
            total = sum(len(v) for v in sched.values())
            done = 0
            stuck = False
            while done < total:
                progress = False
                for ph, recl in sched.items():
                    while pos[ph] < len(recl):
                        eng, idx, waits, inc, _ = recl[pos[ph]]
                        if any(cnt.get(k, 0) < v for k, v in waits):
                            break
                        if inc is not None:
                            cnt[inc] = cnt.get(inc, 0) + \
                                (16 if isinstance(inc, tuple) else 1)
                        pos[ph] += 1
                        done += 1
                        progress = True
                if not progress:
                    stuck = True
                    break
            if not stuck:
                break
            added = 0
            for ph, recl in sched.items():
                if pos[ph] >= len(recl):
                    continue
                eng, idx, waits, _, _ = recl[pos[ph]]
                if idx < 0:
                    continue
                _, dl, _ch = self.recs[eng][idx]
                for dep in dl:
                    if (len(dep) == 3 and dep[2]
                            and (eng, idx, dep[1]) not in noshift):
                        noshift.add((eng, idx, dep[1]))
                        added += 1
            if added == 0:
                msg = ["DEADLOCK not caused by drain shifts; blocked:"]
                for ph, recl in sched.items():
                    if pos[ph] < len(recl):
                        eng, idx, waits, _, _ = recl[pos[ph]]
                        bl = [(k, v, cnt.get(k, 0)) for k, v in waits
                              if cnt.get(k, 0) < v]
                        msg.append(f"  {ph}: {eng}[{idx}] needs {bl}")
                raise AssertionError("\n".join(msg))
        else:
            raise AssertionError("shift revert did not converge")

        self.shift_fall = {}
        nshift = 0
        for e in ENGS:
            for i, (fn, dl, _ch) in enumerate(self.recs[e]):
                for dep in dl:
                    if len(dep) == 3 and dep[2]:
                        if eff(e, i, dep)[1] == dep[1]:
                            ln = (getattr(fn, "__code__", None)
                                  and fn.__code__.co_firstlineno)
                            self.shift_fall[(e, ln)] = \
                                self.shift_fall.get((e, ln), 0) + 1
                        else:
                            nshift += 1
        self.shift_n = nshift

        streams = {p: [] for p in set(PHYS.values())}
        semmap = dict(self.sems)
        for c, s in chan_sems.items():
            semmap[("c", c)] = s

        def mk_thunk(eng, emit_fn, waits, inc):
            def thunk(engobj):
                for k, v in waits:
                    engobj.wait_ge(semmap[k], v)
                inst = emit_fn(engobj)
                if inc is not None:
                    inst.then_inc(semmap[inc],
                                  16 if isinstance(inc, tuple) else 1)
            return thunk

        def mk_wait(waits):
            def thunk(engobj):
                for k, v in waits:
                    engobj.wait_ge(semmap[k], v)
            return thunk

        for ph, recl in sched.items():
            for eng, idx, waits, inc, emit_fn in recl:
                if emit_fn is None:
                    streams[ph].append(mk_wait(waits))
                else:
                    streams[ph].append(mk_thunk(eng, emit_fn, waits, inc))
        return streams


def build_nc():
    nc = bass.Bass("TRN2", target_bir_lowering=False)

    d_x = nc.dram_tensor("x_in", [SEQ_PER_CORE, S, D], F32, kind="ExternalInput").ap()
    d_out = nc.dram_tensor("x_out", [SEQ_PER_CORE, S, D], F32, kind="ExternalOutput").ap()
    dw = {}
    for nm, shp in [
        ("aWq", [D, D]), ("aWk", [D, D]), ("aWv", [D, D]), ("aWd", [D, D]),
        ("aBq", [128, 8]), ("aBk", [128, 8]), ("aBv_row", [1, D]), ("aBd", [128, 8]),
        ("qkvT", [D, 3 * D]), ("qkvB", [128, 24]), ("bBv_row", [1, D]),
        ("outWT", [D, D]), ("outB", [128, 8]),
        ("w1T", [D, HFF]), ("b1", [128, 32]),
        ("w2T", [HFF, D]), ("b2", [128, 8]),
        ("identF", [128, 128]), ("identB", [128, 128]),
        ("cosT", [128, S]), ("sinT", [128, S]),
        ("maskA", [128, 640]),
    ]:
        dt = BF16 if nm in ("cosT", "sinT", "maskA", "identB") else F32
        dw[nm] = nc.dram_tensor(nm, shp, dt, kind="ExternalInput").ap()
    d_w2q = nc.dram_tensor("w2q_scr", [HFF, D], BF16).ap()
    WBF_SHAPES = {"aWq": D, "aWk": D, "aWv": D, "aWd": D,
                  "qkvT": 3 * D, "outWT": D, "w1T": HFF}
    d_wb = {nm: nc.dram_tensor("wb_" + nm, [D, F], BF16).ap()
            for nm, F in WBF_SHAPES.items()}
    d_taps = None
    if DEBUG_TAPS:
        d_taps = nc.dram_tensor("taps", [4, SEQ_PER_CORE, S, D], F32,
                                kind="ExternalOutput").ap()

    ctx = ExitStack()

    def sbuf(name, shape, dt):
        return ctx.enter_context(nc.sbuf_tensor(name, shape, dt))

    def psum(name, shape, dt):
        return ctx.enter_context(nc.psum_tensor(name, shape, dt))

    RES = sbuf("RES", [128, NT * D], F32)
    XLNT = sbuf("XLNT", [128, NC * S], BF16)
    KT = sbuf("KT", [128, NC * S], BF16)
    QH = sbuf("QH", [128, NC * TH], BF16)
    VST = sbuf("VST", [128, NT * D], BF16)
    SHR = sbuf("SHR", [128, 4096], BF16)
    ATO = sbuf("ATO", [128, 1024], BF16)
    SCR = sbuf("SCR", [128, 1024], F32)
    F8S = sbuf("F8S", [128, 2048], FP8)
    F16S = sbuf("F16S", [128, 1024], FP16)
    STT = sbuf("STT", [128, 64], F32)
    ST2 = sbuf("ST2", [128, 8], F32)
    IDF = sbuf("IDF", [128, 128], F32)
    IDB = sbuf("IDB", [128, 128], BF16)
    BIA = {nm: sbuf("s_" + nm, list(dw[nm].shape), F32)
           for nm in ("aBq", "aBk", "aBd", "qkvB", "outB", "b1", "b2")}
    BVR = sbuf("BVR", [33, D], BF16)
    B2QF = sbuf("B2QF", [128, 8], F32)
    B2Q16 = sbuf("B2Q16", [128, 8], FP16)
    ONEB = sbuf("ONEB", [33, 128], BF16)
    MSKA = sbuf("MSKA", [128, 640], BF16)
    COS = sbuf("COS", [128, S], BF16)
    SIN = sbuf("SIN", [128, S], BF16)
    EPS = sbuf("EPS", [128, 1], F32)
    FNC = sbuf("FNC", [128, 4], F32)

    P4A = psum("P4A", [128, 2048], F32)
    PTB = psum("PTB", [128, 1024], BF16)
    PAV = psum("PAV", [128, 512], F32)
    PSC = psum("PSC", [128, 1024], F32)

    sems = {e: ctx.enter_context(nc.semaphore("sm_" + e)) for e in ENGS}
    b = Bld(nc, sems)
    b.dummy = {
        "pe": (lambda e: nc.tensor.ldweights(IDB[:]), (("IDB",),)),
        "act": (lambda e: e.activation(out=FNC[:, 0:1], in_=EPS[:],
                                       func=AF.Identity), (("EPS",),)),
        "dve": (lambda e: e.tensor_copy(out=FNC[:, 1:2], in_=EPS[:]),
                (("EPS",),)),
        "pool": (lambda e: e.tensor_copy(out=FNC[:, 2:3], in_=EPS[:]),
                 (("EPS",),)),
    }

    def pa(i):
        return ("P4A", i)

    def shr(i):
        return SHR[:, 1024 * i:1024 * (i + 1)]

    def SH(i):
        return ("SHR", i)

    def emit_setup():
        b.op("sync", lambda e: e.dma_start(out=IDF[:], in_=dw["identF"]),
             writes=[("IDF",)], chan=("k", "IDF"))
        b.op("sync", lambda e: e.dma_start(out=IDB[:], in_=dw["identB"]),
             writes=[("IDB",)], chan=("k", "IDB"))
        b.op("sync", lambda e: e.dma_start(out=MSKA[:], in_=dw["maskA"]),
             writes=[("MSKA",)], chan=("k", "MSKA"))
        for nm in BIA:
            b.op("sync", lambda e, nm=nm: e.dma_start(out=BIA[nm][:], in_=dw[nm]),
                 writes=[("BIA", nm)], chan=("k", nm))
        b.op("gdma", lambda e: e.dma_start(out=BVR[0:1, 0:D], in_=dw["aBv_row"]),
             writes=[("BVR",)], chan=("k", "BVR"))
        b.op("gdma", lambda e: e.dma_start(out=BVR[32:33, 0:D], in_=dw["bBv_row"]),
             writes=[("BVR",)], chan=("k", "BVR"))
        b.op("dve", lambda e: e.memset(ONEB[:], 1.0), writes=[("ONEB",)])
        b.op("dve", lambda e: e.memset(EPS[:], 1e-5), writes=[("EPS",)])
        b.op("dve", lambda e: e.tensor_copy(out=B2Q16[:], in_=BIA["b2"][:]),
             reads=[("BIA", "b2")], writes=[("B2Q16",)])
        b.op("dve", lambda e: e.tensor_copy(out=B2QF[:], in_=B2Q16[:]),
             reads=[("B2Q16",)], writes=[("B2QF",)])
        b.op("sync", lambda e: e.dma_start(out=COS[:], in_=dw["cosT"]),
             writes=[("COS",)], chan=("k", "COS"))
        b.op("sync", lambda e: e.dma_start(out=SIN[:], in_=dw["sinT"]),
             writes=[("SIN",)], chan=("k", "SIN"))
        for r in range(32):
            i = r % 2
            src = dw["w2T"][128 * r:128 * (r + 1), :]
            b.op("gdma", lambda e, src=src, i=i: e.dma_start(
                out=RES[:, 1024 * i:1024 * (i + 1)], in_=src),
                writes=[("RES", i)], chan=("xl", i))
            f8 = F8S[:, 1024 * i:1024 * (i + 1)]
            b.op("act", lambda e, i=i, f8=f8: e.activation(
                out=f8, in_=RES[:, 1024 * i:1024 * (i + 1)], func=AF.Identity),
                reads=[("RES", i)], writes=[("F8S", i)])
            b.op("act", lambda e, i=i, f8=f8: e.activation(out=shr(i), in_=f8,
                                                           func=AF.Identity),
                 reads=[("F8S", i)], writes=[SH(i)])
            b.op("sync", lambda e, r=r, i=i: e.dma_start(
                out=d_w2q[128 * r:128 * (r + 1), :], in_=shr(i)),
                reads=[SH(i)], writes=[("W2QD",)], chan=("dwq",))
        ci = 0
        for nm, F in WBF_SHAPES.items():
            for rr in range(NC):
                for cc in range(F // 1024):
                    i = ci % 4
                    ci += 1
                    srcc = dw[nm][128 * rr:128 * (rr + 1), 1024 * cc:1024 * (cc + 1)]
                    b.op("gdma", lambda e, srcc=srcc, i=i: e.dma_start(
                        out=RES[:, 1024 * i:1024 * (i + 1)], in_=srcc),
                        writes=[("RES", i)], chan=("xl", i))
                    b.op("act", lambda e, i=i: e.activation(
                        out=shr(i), in_=RES[:, 1024 * i:1024 * (i + 1)],
                        func=AF.Identity),
                        reads=[("RES", i)], writes=[SH(i)])
                    b.op("sync", lambda e, nm=nm, rr=rr, cc=cc, i=i: e.dma_start(
                        out=d_wb[nm][128 * rr:128 * (rr + 1),
                                     1024 * cc:1024 * (cc + 1)], in_=shr(i)),
                        reads=[SH(i)], writes=[("WB", nm)], chan=("dwb", nm))

    def emit_ln():
        for t in range(NT):
            rt = RES[:, D * t:D * (t + 1)]
            b.op("act", lambda e, rt=rt, t=t: e.activation(
                out=shr(t % 2), in_=rt, func=AF.Square,
                accum_out=STT[:, 16 + t:17 + t]),
                reads=[("RES", t)], writes=[SH(t % 2), ("STT", "sq", t)])
            b.op("dve", lambda e, rt=rt, t=t: e.reduce_sum(
                out=STT[:, t:t + 1], in_=rt, axis=AX.X),
                reads=[("RES", t)], writes=[("STT", "sum", t)])
        for t in range(NT):
            b.op("dve", lambda e, t=t: e.tensor_scalar(
                out=STT[:, 32 + t:33 + t], in0=STT[:, t:t + 1],
                scalar1=1.0 / D, scalar2=None, op0=ALU.mult),
                reads=[("STT", "sum", t)], writes=[("STT", "m", t)])
            b.op("dve", lambda e, t=t: e.tensor_scalar(
                out=STT[:, 48 + t:49 + t], in0=STT[:, 32 + t:33 + t],
                scalar1=STT[:, 32 + t:33 + t], scalar2=None, op0=ALU.mult),
                reads=[("STT", "m", t)], writes=[("STT", "v", t)])
            b.op("dve", lambda e, t=t: e.scalar_tensor_tensor(
                out=STT[:, 48 + t:49 + t], in0=STT[:, 16 + t:17 + t],
                scalar=1.0 / D, in1=STT[:, 48 + t:49 + t],
                op0=ALU.mult, op1=ALU.subtract),
                reads=[("STT", "sq", t), ("STT", "v", t)], writes=[("STT", "v", t)])
        for t in range(NT):
            b.op("act", lambda e, t=t: e.activation(
                out=STT[:, 48 + t:49 + t], in_=STT[:, 48 + t:49 + t],
                func=AF.Ln, bias=EPS[:, 0:1]),
                reads=[("STT", "v", t), ("EPS",)], writes=[("STT", "v", t)])
            b.op("act", lambda e, t=t: e.activation(
                out=STT[:, 48 + t:49 + t], in_=STT[:, 48 + t:49 + t],
                func=AF.Exp, scale=-0.5),
                reads=[("STT", "v", t)], writes=[("STT", "v", t)])
        for t in range(NT):
            rt = RES[:, D * t:D * (t + 1)]
            scr2 = SCR[:, 0:1024] if t % 2 == 0 else SCR[:, 0:1024]
            b.op("dve", lambda e, rt=rt, t=t: e.tensor_scalar(
                out=SCR[:], in0=rt, scalar1=STT[:, 32 + t:33 + t],
                scalar2=STT[:, 48 + t:49 + t], op0=ALU.subtract, op1=ALU.mult),
                reads=[("RES", t), ("STT", "m", t), ("STT", "v", t)],
                writes=[("SCR",)])
            off = 0 if t % 2 == 0 else 1024
            pks = [pa(2 * (t % 2)), pa(2 * (t % 2) + 1)]
            for j in range(NC):
                b.op("pe", lambda e, j=j, off=off: nc.tensor.transpose(
                    out=P4A[:, off + 128 * j:off + 128 * (j + 1)],
                    in_=SCR[:, 128 * j:128 * (j + 1)], identity=IDF[:]),
                    reads=[("SCR",), ("IDF",)], writes=[pks[j // 4]])
            dst = XLNT[:].rearrange("p (c x) -> p c x", c=NC)[:, :, 128 * t:128 * (t + 1)]
            b.op("act", lambda e, dst=dst, off=off: e.activation(
                out=dst, in_=P4A[:, off:off + 1024], func=AF.Identity),
                reads=pks, writes=[("XLNT", t)])

    def emit_proj_fm(W_ap, f_list, dst_sb, dst_stride, t0, tn, bias_nm, dst_tag):
        nmm = tn // 512
        tkeys = [("XLNT", t) for t in range(t0 // 128, (t0 + tn) // 128)]
        for i, (fc_src, fc_dst) in enumerate(f_list):
            si = i % 2
            src = W_ap[:, 128 * fc_src:128 * (fc_src + 1)].rearrange(
                "(j p) c -> p j c", p=128)
            b.op("gdma", lambda e, src=src, si=si: e.dma_start(
                out=shr(si)[:].rearrange("p (j c) -> p j c", j=NC), in_=src),
                writes=[SH(si)], chan=("shr", si))
            for d in range(NC):
                rhs = XLNT[:, S * d + t0:S * d + t0 + tn]
                for m in range(nmm):
                    b.op("pe", lambda e, si=si, d=d, m=m, rhs=rhs,
                         st_=(d == 0), sp_=(d == NC - 1): nc.tensor.matmul(
                        out=P4A[:, 512 * m:512 * (m + 1)],
                        lhsT=shr(si)[:, 128 * d:128 * (d + 1)],
                        rhs=rhs[:, 512 * m:512 * (m + 1)],
                        start=st_, stop=sp_),
                        reads=[SH(si)] + tkeys, writes=[pa(m)])
            for m in range(nmm):
                dst = dst_sb[:, dst_stride * fc_dst + 512 * m:
                             dst_stride * fc_dst + 512 * (m + 1)]
                bia = BIA[bias_nm][:, fc_src:fc_src + 1]
                if dst_tag == "QH":
                    wkeys = [("QH", fc_dst, q) for q in range(4 * m, 4 * m + 4)]
                else:
                    wkeys = [(dst_tag, fc_dst)]
                b.op("act", lambda e, dst=dst, m=m, bia=bia: e.activation(
                    out=dst, in_=P4A[:, 512 * m:512 * (m + 1)], func=AF.Identity,
                    bias=bia),
                    reads=[pa(m), ("BIA", bias_nm)], writes=wkeys)

    def emit_vproj(Wv_ap, bvr_off):
        for tg in range(NT // 2):
            pk = [[("PSC",)], [pa(0), pa(1)]]
            pss = [PSC, P4A]
            for d in range(NC):
                si = 2 + d % 2
                src = Wv_ap[128 * d:128 * (d + 1), :]
                b.op("gdma", lambda e, src=src, si=si: e.dma_start(out=shr(si), in_=src),
                     writes=[SH(si)], chan=("shr", si))
                for ti in range(2):
                    t = 2 * tg + ti
                    lhs = XLNT[:].rearrange("p (c x) -> p c x", c=NC)[
                        :, d, 128 * t:128 * (t + 1)]
                    ps = pss[ti]
                    for m in range(2):
                        b.op("pe", lambda e, lhs=lhs, si=si, ps=ps, m=m, d=d:
                             nc.tensor.matmul(
                                 out=ps[:, 512 * m:512 * (m + 1)], lhsT=lhs,
                                 rhs=shr(si)[:, 512 * m:512 * (m + 1)],
                                 start=(d == 0), stop=False),
                             reads=[SH(si), ("XLNT", 2 * tg + ti)],
                             writes=pk[ti])
            for ti in range(2):
                ps = pss[ti]
                for m in range(2):
                    b.op("pe", lambda e, ps=ps, m=m: nc.tensor.matmul(
                        out=ps[:, 512 * m:512 * (m + 1)],
                        lhsT=ONEB[bvr_off:bvr_off + 1, :],
                        rhs=BVR[bvr_off:bvr_off + 1, 512 * m:512 * (m + 1)],
                        start=False, stop=True),
                        reads=[("ONEB",), ("BVR",)], writes=pk[ti])
            for ti in range(2):
                t = 2 * tg + ti
                ps = pss[ti]
                dst = VST[:, 1024 * t:1024 * (t + 1)]
                b.op("act", lambda e, dst=dst, ps=ps: e.activation(
                    out=dst, in_=ps[:, 0:1024], func=AF.Identity),
                    reads=pk[ti], writes=[("VST", t)])

    def emit_rope(buf_sb, col0, key, cs_off, par=0):
        keys = key if isinstance(key, list) else [key]
        x = buf_sb[:, col0:col0 + 1024]
        s0, s1 = (2, 3) if par % 2 == 0 else (0, 1)
        swx, a_ = shr(s0), shr(s1)
        cbuf = COS[:, cs_off:cs_off + 1024]
        sbuf_ = SIN[:, cs_off:cs_off + 1024]
        for blk in range(4):
            srcp = 32 * (blk + 1) if blk % 2 == 0 else 32 * (blk - 1)
            b.op("pool", lambda e, srcp=srcp, blk=blk, x=x, swx=swx: e.tensor_copy(
                out=swx[32 * blk:32 * blk + 32, :], in_=x[srcp:srcp + 32, :]),
                reads=keys, writes=[SH(s0)])
        b.op("dve", lambda e, x=x, a_=a_, cbuf=cbuf: e.tensor_mul(out=a_, in0=x,
                                                                  in1=cbuf),
             reads=keys + [("COS",)], writes=[SH(s1)])
        b.op("dve", lambda e, swx=swx, sbuf_=sbuf_: e.tensor_mul(out=swx, in0=swx,
                                                                 in1=sbuf_),
             reads=[SH(s0), ("SIN",)], writes=[SH(s0)])
        b.op("dve", lambda e, x=x, a_=a_, swx=swx: e.tensor_add(out=x, in0=a_,
                                                                in1=swx),
             reads=[SH(s1), SH(s0)], writes=keys)

    SC_R = [(0, 1), (2, 3), (6, 7)]

    def emit_mixer(hh, banded):
        rot = [0]

        def sc_region(r):
            if r == 0:
                return P4A[:, 0:1024], [pa(0), pa(1)]
            if r == 1:
                return P4A[:, 1024:2048], [pa(2), pa(3)]
            return PSC[:, 0:1024], [("PSC",)]

        def geom(qt):
            qtg = 8 * hh + qt
            if banded:
                k0 = max(0, 128 * qtg - WINDOW)
                k1 = min(S, 128 * qtg + 384)
            else:
                k0, k1 = 0, S
            return k0, k1 - k0

        items = []
        for qt in range(8):
            k0, w = geom(qt)
            nh = (w + 1023) // 1024
            for h in range(NH):
                for m in range(nh):
                    kk0 = k0 + 1024 * m
                    ww = min(1024, k0 + w - kk0)
                    items.append((qt, h, m, kk0, ww, m == nh - 1))
        nit = len(items)
        regs = [None] * nit

        def stage_scores(n):
            qt, h, m, kk0, ww, last = items[n]
            r = rot[0] % 3
            rot[0] += 1
            regs[n] = r
            ps, sk = sc_region(r)
            ch, sub = h // 2, h % 2
            lhsq = QH[:, TH * ch + 128 * qt:TH * ch + 128 * (qt + 1)][
                64 * sub:64 * sub + 64, :]
            for m0 in range(0, ww, 512):
                m1 = min(m0 + 512, ww)
                rhs = KT[:, S * ch + kk0 + m0:S * ch + kk0 + m1][
                    64 * sub:64 * sub + 64, :]
                b.op("pe", lambda e, lhsq=lhsq, rhs=rhs, m0=m0, m1=m1, ps=ps:
                     nc.tensor.matmul(out=ps[:, m0:m1], lhsT=lhsq, rhs=rhs,
                                      start=True, stop=True),
                     reads=[("KT", ch), ("QH", ch, qt)], writes=sk)
            if banded:
                qtg = 8 * hh + qt
                moff = kk0 - (128 * qtg - WINDOW)
                b.op("dve", lambda e, ww=ww, ps=ps, moff=moff: e.tensor_add(
                    out=ps[:, 0:ww], in0=ps[:, 0:ww],
                    in1=MSKA[:, moff:moff + ww]),
                    reads=sk + [("MSKA",)], writes=sk)

        def stage_max(n):
            qt, h, m, kk0, ww, last = items[n]
            ps, sk = sc_region(regs[n])
            b.op("dve", lambda e, ww=ww, ps=ps, m=m: e.reduce_max(
                out=ST2[:, m:m + 1], in_=ps[:, 0:ww], axis=AX.X),
                reads=sk, writes=[("ST2", "mx", m)])

        def stage_comb(n):
            qt, h, m, kk0, ww, last = items[n]
            if m == 0:
                b.op("dve", lambda e: e.tensor_scalar(
                    out=ST2[:, 2:3], in0=ST2[:, 0:1], scalar1=-0.125,
                    scalar2=None, op0=ALU.mult),
                    reads=[("ST2", "mx", 0)], writes=[("ST2", "mb")])
            else:
                b.op("dve", lambda e: e.tensor_max(
                    out=ST2[:, 0:1], in0=ST2[:, 0:1], in1=ST2[:, 1:2]),
                    reads=[("ST2", "mx", 0), ("ST2", "mx", 1)],
                    writes=[("ST2", "mx", 0)])
                b.op("dve", lambda e: e.tensor_scalar(
                    out=ST2[:, 2:3], in0=ST2[:, 0:1], scalar1=-0.125,
                    scalar2=None, op0=ALU.mult),
                    reads=[("ST2", "mx", 0)], writes=[("ST2", "mb")])

        def stage_exp(n):
            qt, h, m, kk0, ww, last = items[n]
            ps, sk = sc_region(regs[n])
            es = n % 2
            ek = SH(es)
            ebuf = shr(es)
            zc = 4 + 2 * gpar[n] + m
            b.op("act", lambda e, ww=ww, ps=ps, ebuf=ebuf, zc=zc: e.activation(
                out=ebuf[:, 0:ww], in_=ps[:, 0:ww], func=AF.Exp,
                scale=0.125, bias=ST2[:, 2:3], accum_out=ST2[:, zc:zc + 1]),
                reads=sk + [("ST2", "mb")], writes=[ek, ("ST2", "z", gpar[n], m)])

        def stage_tr(n):
            qt, h, m, kk0, ww, last = items[n]
            es = n % 2
            ek = SH(es)
            ebuf = shr(es)
            nkb = ww // 128
            for j in range(nkb):
                b.op("pe", lambda e, j=j, ebuf=ebuf: nc.tensor.transpose(
                    out=PTB[:, 128 * j:128 * (j + 1)],
                    in_=ebuf[:, 128 * j:128 * (j + 1)], identity=IDB[:]),
                    reads=[ek, ("IDB",)], writes=[("PTB",)])
            pt = 2 + n % 2
            ptk = SH(pt)
            ptbuf = shr(pt)
            if n % 2:
                b.op("act", lambda e, nkb=nkb, ptbuf=ptbuf: e.activation(
                    out=ptbuf[:, 0:128 * nkb], in_=PTB[:, 0:128 * nkb],
                    func=AF.Identity),
                    reads=[("PTB",)], writes=[ptk])
            else:
                b.op("dve", lambda e, nkb=nkb, ptbuf=ptbuf: e.tensor_copy(
                    out=ptbuf[:, 0:128 * nkb], in_=PTB[:, 0:128 * nkb]),
                    reads=[("PTB",)], writes=[ptk])

        def stage_av(n):
            qt, h, m, kk0, ww, last = items[n]
            nkb = ww // 128
            pt = 2 + n % 2
            ptk = SH(pt)
            ptbuf = shr(pt)
            for i in range(nkb):
                kt = kk0 // 128 + i
                lhs = ptbuf[:, 128 * i:128 * (i + 1)]
                rhs = VST[:, 1024 * kt + 64 * h:1024 * kt + 64 * h + 64]
                b.op("pe", lambda e, lhs=lhs, rhs=rhs,
                     st_=(m == 0 and i == 0),
                     sp_=(last and i == nkb - 1): nc.tensor.matmul(
                    out=PAV[:, 0:64], lhsT=lhs, rhs=rhs, start=st_, stop=sp_),
                    reads=[ptk, ("VST", kt)], writes=[("PAV",)])

        def stage_norm(n):
            qt, h, m, kk0, ww, last = items[n]
            gp = gpar[n]
            z0 = 4 + 2 * gp
            if m == 1:
                b.op("act", lambda e, z0=z0: e.activation(
                    out=ST2[:, z0:z0 + 1], in_=ST2[:, z0:z0 + 1],
                    func=AF.Identity, bias=ST2[:, z0 + 1:z0 + 2]),
                    reads=[("ST2", "z", gp, 0), ("ST2", "z", gp, 1)],
                    writes=[("ST2", "z", gp, 0)])
            b.op("dve", lambda e, z0=z0: e.reciprocal(out=ST2[:, 3:4],
                                                      in_=ST2[:, z0:z0 + 1]),
                 reads=[("ST2", "z", gp, 0)], writes=[("ST2", "rz")])
            b.op("act", lambda e, h=h: e.activation(
                out=ATO[:, 64 * h:64 * (h + 1)], in_=PAV[:, 0:64],
                func=AF.Identity, scale=ST2[:, 3:4]),
                reads=[("PAV",), ("ST2", "rz")], writes=[("ATO",)])

        def stage_ato(qt):
            for j in range(NC):
                b.op("pe", lambda e, j=j: nc.tensor.transpose(
                    out=PTB[:, 128 * j:128 * (j + 1)],
                    in_=ATO[:, 128 * j:128 * (j + 1)], identity=IDB[:]),
                    reads=[("ATO",), ("IDB",)], writes=[("PTB",)])
            dst = QH[:].rearrange("p (c x) -> p c x", c=NC)[
                :, :, 128 * qt:128 * (qt + 1)]
            b.op("act", lambda e, dst=dst: e.activation(
                out=dst, in_=PTB[:], func=AF.Identity),
                reads=[("PTB",)],
                writes=[("QH", j, qt) for j in range(NC)])

        groups = []
        gpar = {}
        i = 0
        while i < nit:
            j = i
            while not items[j][5]:
                j += 1
            groups.append(list(range(i, j + 1)))
            for n_ in range(i, j + 1):
                gpar[n_] = (len(groups) - 1) % 2
            i = j + 1
        for n in groups[0]:
            stage_scores(n)
        avq = []
        normq = []
        for gi, grp in enumerate(groups):
            for n in grp:
                stage_max(n)
            for n in grp:
                stage_comb(n)
            nxt = groups[gi + 1] if gi + 1 < len(groups) else []
            for k, n in enumerate(grp):
                if k < len(nxt):
                    stage_scores(nxt[k])
                stage_exp(n)
                stage_tr(n)
                if avq:
                    stage_av(avq.pop(0))
                avq.append(n)
                while normq and items[normq[0]][5] and normq[0] not in avq:
                    nl = normq.pop(0)
                    stage_norm(nl)
                    qtl, hl = items[nl][0], items[nl][1]
                    if hl == NH - 1:
                        stage_ato(qtl)
            for k in range(len(grp), len(nxt)):
                stage_scores(nxt[k])
            normq.append(grp[-1])
        while avq:
            stage_av(avq.pop(0))
        while normq:
            nl = normq.pop(0)
            stage_norm(nl)
            qtl, hl = items[nl][0], items[nl][1]
            if hl == NH - 1:
                stage_ato(qtl)

    def emit_outproj(hh, W_ap, bias_nm):
        for g in range(NC):
            si = 2 + g % 2
            src = W_ap[:, 128 * g:128 * (g + 1)].rearrange("(j p) c -> p j c", p=128)
            b.op("gdma", lambda e, src=src, si=si: e.dma_start(
                out=shr(si)[:].rearrange("p (j c) -> p j c", j=NC), in_=src),
                writes=[SH(si)], chan=("shr", si))
            for d in range(NC):
                rhs = QH[:, TH * d:TH * (d + 1)]
                for m in range(2):
                    b.op("pe", lambda e, si=si, d=d, m=m, rhs=rhs: nc.tensor.matmul(
                        out=PSC[:, 512 * m:512 * (m + 1)],
                        lhsT=shr(si)[:, 128 * d:128 * (d + 1)],
                        rhs=rhs[:, 512 * m:512 * (m + 1)],
                        start=(d == 0), stop=(d == NC - 1)),
                        reads=[SH(si)] + [("QH", d, q) for q in range(8)],
                        writes=[("PSC",)])
            bia = BIA[bias_nm][:, g:g + 1]
            b.op("act", lambda e, bia=bia: e.activation(
                out=SCR[:], in_=PSC[:], func=AF.Identity, bias=bia),
                reads=[("PSC",), ("BIA", bias_nm)], writes=[("SCR",)])
            off = 0 if g % 2 == 0 else 1024
            opk = [pa(2 * (g % 2)), pa(2 * (g % 2) + 1)]
            for j in range(NC):
                b.op("pe", lambda e, j=j, off=off: nc.tensor.transpose(
                    out=P4A[:, off + 128 * j:off + 128 * (j + 1)],
                    in_=SCR[:, 128 * j:128 * (j + 1)], identity=IDF[:]),
                    reads=[("SCR",), ("IDF",)], writes=opk)
            rview = RES[:].rearrange("p (t x) -> p t x", t=NT)[
                :, 8 * hh:8 * hh + 8, 128 * g:128 * (g + 1)]
            pview = P4A[:, off:off + 1024].rearrange("p (t x) -> p t x", t=8)
            b.op("dve", lambda e, rview=rview, pview=pview: e.tensor_add(
                out=rview, in0=rview, in1=pview),
                reads=opk + [("RES", 8 * hh + j) for j in range(8)],
                writes=[("RES", 8 * hh + j) for j in range(8)])

    def hq_slot(hc):
        if hc < 16:
            return KT, hc, ("KT", hc)
        return VST, hc - 16, ("VST", "hq", hc - 16)

    def emit_ffn():
        for tc in range(2):
            t0 = TH * tc
            tkeys = [("XLNT", t) for t in range(8 * tc, 8 * tc + 8)]
            for hc in range(32):
                si = 2 + hc % 2
                src = d_wb["w1T"][:, 128 * hc:128 * (hc + 1)].rearrange(
                    "(j p) c -> p j c", p=128)
                b.op("gdma", lambda e, src=src, si=si: e.dma_start(
                    out=shr(si)[:].rearrange("p (j c) -> p j c", j=NC), in_=src),
                    writes=[SH(si)], chan=("shr", si))
                pk = [("PSC",)] if hc % 2 == 0 else [pa(0), pa(1)]
                ps = PSC if hc % 2 == 0 else P4A
                for d in range(NC):
                    rhs = XLNT[:, S * d + t0:S * d + t0 + TH]
                    for m in range(2):
                        b.op("pe", lambda e, si=si, d=d, m=m, rhs=rhs, ps=ps:
                             nc.tensor.matmul(
                                 out=ps[:, 512 * m:512 * (m + 1)],
                                 lhsT=shr(si)[:, 128 * d:128 * (d + 1)],
                                 rhs=rhs[:, 512 * m:512 * (m + 1)],
                                 start=(d == 0), stop=(d == NC - 1)),
                             reads=[SH(si)] + tkeys, writes=pk)
                f8 = F8S[:, 1024 * (hc % 2):1024 * (hc % 2 + 1)]
                fk = ("F8S", hc % 2)
                b.op("act", lambda e, ps=ps, f8=f8, hc=hc: e.activation(
                    out=f8, in_=ps[:, 0:1024], func=AF.Relu,
                    bias=BIA["b1"][:, hc:hc + 1]),
                    reads=pk + [("BIA", "b1")], writes=[fk])
                hq, hqc, hqk = hq_slot(hc)
                dst = hq[:, 1024 * hqc:1024 * (hqc + 1)]
                b.op("act", lambda e, f8=f8, dst=dst: e.activation(
                    out=dst, in_=f8, func=AF.Identity),
                    reads=[fk], writes=[hqk])
            for g in range(NC):
                for half in range(2):
                    src = d_w2q[2048 * half:2048 * (half + 1),
                                128 * g:128 * (g + 1)].rearrange(
                        "(j p) c -> p j c", p=128)
                    dsts = SHR[:, 2048 * half:2048 * (half + 1)]
                    hkeys = [SH(2 * half), SH(2 * half + 1)]
                    b.op("gdma", lambda e, src=src, dsts=dsts: e.dma_start(
                        out=dsts[:].rearrange("p (j c) -> p j c", j=16), in_=src),
                        reads=[("W2QD",)], writes=hkeys, chan=("shr", 2 * half))
                pk = [("PSC",)] if g % 2 == 0 else [pa(0), pa(1)]
                ps = PSC if g % 2 == 0 else P4A
                for hc in range(32):
                    half = hc // 16
                    lt = SHR[:, 2048 * half + 128 * (hc % 16):
                             2048 * half + 128 * (hc % 16) + 128]
                    hq, hqc, hqk = hq_slot(hc)
                    rhs = hq[:, 1024 * hqc:1024 * (hqc + 1)]
                    for m in range(2):
                        b.op("pe", lambda e, lt=lt, m=m, rhs=rhs, ps=ps, hc=hc:
                             nc.tensor.matmul(
                                 out=ps[:, 512 * m:512 * (m + 1)], lhsT=lt,
                                 rhs=rhs[:, 512 * m:512 * (m + 1)],
                                 start=(hc == 0), stop=(hc == 31)),
                             reads=[SH(2 * half), SH(2 * half + 1), hqk],
                             writes=pk)
                b.op("act", lambda e, ps=ps, g=g: e.activation(
                    out=F16S[:], in_=ps[:, 0:1024], func=AF.Identity,
                    bias=B2QF[:, g:g + 1]),
                    reads=pk + [("B2QF",)], writes=[("F16S",)])
                b.op("act", lambda e: e.activation(out=SCR[:], in_=F16S[:],
                                                   func=AF.Identity),
                     reads=[("F16S",)], writes=[("SCR",)])
                off = 1024 if g % 2 == 0 else 0
                opk = [pa(2 * (1 - g % 2)), pa(2 * (1 - g % 2) + 1)]
                for j in range(NC):
                    b.op("pe", lambda e, j=j, off=off: nc.tensor.transpose(
                        out=P4A[:, off + 128 * j:off + 128 * (j + 1)],
                        in_=SCR[:, 128 * j:128 * (j + 1)], identity=IDF[:]),
                        reads=[("SCR",), ("IDF",)], writes=opk)
                rview = RES[:].rearrange("p (t x) -> p t x", t=NT)[
                    :, 8 * tc:8 * tc + 8, 128 * g:128 * (g + 1)]
                pview = P4A[:, off:off + 1024].rearrange("p (t x) -> p t x", t=8)
                b.op("dve", lambda e, rview=rview, pview=pview: e.tensor_add(
                    out=rview, in0=rview, in1=pview),
                    reads=opk + [("RES", 8 * tc + j) for j in range(8)],
                    writes=[("RES", 8 * tc + j) for j in range(8)])

    def emit_attn(which):
        banded = which == "a"
        if banded:
            kW, kBias = d_wb["aWk"], "aBk"
            qW, qBias = d_wb["aWq"], "aBq"
            vW, bvro = d_wb["aWv"], 0
            oW, oBias = d_wb["aWd"], "aBd"
            kf = [(c, c) for c in range(NC)]
            qf = [(c, c) for c in range(NC)]
        else:
            kW, kBias = d_wb["qkvT"], "qkvB"
            qW, qBias = d_wb["qkvT"], "qkvB"
            vW, bvro = d_wb["qkvT"][:, 2 * D:3 * D], 32
            oW, oBias = d_wb["outWT"], "outB"
            kf = [(8 + c, c) for c in range(NC)]
            qf = [(c, c) for c in range(NC)]
        emit_proj_fm(kW, kf, KT, S, 0, S, kBias, "KT")
        if not banded:
            for ch in range(NC):
                for half in range(2):
                    emit_rope(KT, S * ch + TH * half, ("KT", ch), TH * half,
                              par=2 * ch + half)
        emit_vproj(vW, bvro)
        for hh in range(2):
            emit_proj_fm(qW, qf, QH, TH, TH * hh, TH, qBias, "QH")
            if not banded:
                for ch in range(NC):
                    emit_rope(QH, TH * ch,
                              [("QH", ch, q) for q in range(8)], TH * hh, par=ch)
            emit_mixer(hh, banded)
            emit_outproj(hh, oW, oBias)

    emit_setup()
    for s in range(SEQ_PER_CORE):
        for t in range(NT):
            b.op("sync", lambda e, s=s, t=t: e.dma_start(
                out=RES[:, D * t:D * (t + 1)], in_=d_x[s, 128 * t:128 * (t + 1), :]),
                writes=[("RES", t)], chan=("xl", t))
        for li, sub in enumerate(("a", "f", "b", "f")):
            emit_ln()
            if sub == "f":
                emit_ffn()
            else:
                emit_attn(sub)
            if DEBUG_TAPS:
                for t in range(NT):
                    b.op("sync", lambda e, s=s, t=t, li=li: e.dma_start(
                        out=d_taps[li, s, 128 * t:128 * (t + 1), :],
                        in_=RES[:, D * t:D * (t + 1)]),
                        reads=[("RES", t)], chan=("xl", t))
        for t in range(NT):
            b.op("sync", lambda e, s=s, t=t: e.dma_start(
                out=d_out[s, 128 * t:128 * (t + 1), :],
                in_=RES[:, D * t:D * (t + 1)]),
                reads=[("RES", t)], chan=("xl", t))
    chan_sems = {c: ctx.enter_context(nc.semaphore("c%d" % i))
                 for i, c in enumerate(sorted(b.chans, key=str))}
    streams = b.finalize(chan_sems)

    with nc.Block() as block:
        @block.sync
        def _(e):
            for th in streams["sync"]:
                th(e)

        @block.gpsimd
        def _(e):
            for th in streams["gpsimd"]:
                th(e)

        @block.vector
        def _(e):
            for th in streams["vector"]:
                th(e)

        @block.scalar
        def _(e):
            for th in streams["scalar"]:
                th(e)

        @block.tensor
        def _(e):
            for th in streams["tensor"]:
                th(e)

    ctx.close()
    return nc


def _host_inputs(inputs):
    f32 = np.float32
    g = {k: np.asarray(v) for k, v in inputs.items()}

    p = np.arange(2 * D)
    for h in range(2 * NH):
        blk = np.arange(HD) + h * HD
        p[h * HD:(h + 1) * HD] = np.concatenate([blk[0::2], blk[1::2]])
    qkvT = np.ascontiguousarray(g["L0_attn_b_qkv_w"].T.astype(f32))
    qkvT = np.concatenate([qkvT[:, :2 * D][:, p], qkvT[:, 2 * D:]], axis=1)
    qkvb = g["L0_attn_b_qkv_b"].astype(f32)
    qkvb = np.concatenate([qkvb[:2 * D][p], qkvb[2 * D:]])

    def brect(v, cols):
        return np.ascontiguousarray(v.astype(f32).reshape(cols, 128).T)

    inv = 1.0 / 10000.0 ** (np.arange(0, HD, 2, dtype=np.float64) / HD)
    th = np.arange(S, dtype=np.float64)[:, None] * inv[None, :]
    cos = np.cos(th).T.astype(f32)
    sin = np.sin(th).T.astype(f32)
    cosT = np.tile(cos, (4, 1)).astype(ml_dtypes.bfloat16)
    sinT = np.concatenate([-sin, sin, -sin, sin], axis=0).astype(ml_dtypes.bfloat16)

    pp = np.arange(128)[:, None]
    cc = np.arange(640)[None, :]
    maskA = np.where((cc >= pp) & (cc <= pp + 2 * WINDOW), 0.0,
                     -30000.0).astype(ml_dtypes.bfloat16)

    w = {
        "aWq": np.ascontiguousarray(g["L0_attn_a_Wq"].astype(f32)),
        "aWk": np.ascontiguousarray(g["L0_attn_a_Wk"].astype(f32)),
        "aWv": np.ascontiguousarray(g["L0_attn_a_Wv"].astype(f32)),
        "aWd": np.ascontiguousarray(g["L0_attn_a_Wd"].astype(f32)),
        "aBq": brect(g["L0_attn_a_bq"], 8),
        "aBk": brect(g["L0_attn_a_bk"], 8),
        "aBv_row": np.ascontiguousarray(g["L0_attn_a_bv"].astype(f32)[None, :]),
        "aBd": brect(g["L0_attn_a_bd"], 8),
        "qkvT": np.ascontiguousarray(qkvT),
        "qkvB": brect(qkvb, 24),
        "bBv_row": np.ascontiguousarray(qkvb[2 * D:][None, :]),
        "outWT": np.ascontiguousarray(g["L0_attn_b_out_w"].T.astype(f32)),
        "outB": brect(g["L0_attn_b_out_b"], 8),
        "w1T": np.ascontiguousarray(g["L0_ffn_w1"].T.astype(f32)),
        "b1": brect(g["L0_ffn_b1"], 32),
        "w2T": np.ascontiguousarray(g["L0_ffn_w2"].T.astype(f32)),
        "b2": brect(g["L0_ffn_b2"], 8),
        "identF": np.eye(128, dtype=f32),
        "identB": np.eye(128, dtype=f32).astype(ml_dtypes.bfloat16),
        "cosT": np.ascontiguousarray(cosT),
        "sinT": np.ascontiguousarray(sinT),
        "maskA": np.ascontiguousarray(maskA),
    }
    x = np.asarray(g["L0_attn_a_x"], dtype=f32)
    return x, w


def kernel(**inputs):
    x, w = _host_inputs(inputs)
    nc = build_nc()
    in_maps = []
    for core in range(NCORES):
        im = {"x_in": np.ascontiguousarray(
            x[SEQ_PER_CORE * core:SEQ_PER_CORE * (core + 1)])}
        im.update(w)
        in_maps.append(im)
    res = run_bass_kernel_spmd(nc, in_maps, core_ids=list(range(NCORES)))
    out = np.concatenate([np.asarray(r["x_out"]) for r in res.results], axis=0)
    if DEBUG_TAPS:
        kernel._taps = np.stack([np.asarray(r["taps"]) for r in res.results])
    return np.ascontiguousarray(out.astype(np.float32))
